# version 21
# baseline (speedup 1.0000x reference)
import contextlib
import numpy as np
import ml_dtypes
import concourse.bass as bass
import concourse.mybir as mybir
from concourse.bass_utils import run_bass_kernel_spmd

F32 = mybir.dt.float32
BF16 = mybir.dt.bfloat16
I32 = mybir.dt.int32
AF = mybir.ActivationFunctionType
ALU = mybir.AluOpType
AX = mybir.AxisListType.X

B, T, D = 4, 4096, 2048
TH = T // 2
TB = 512
NTB = T // TB
NQ = 8 * NTB
NKC = D // 128


def set_config(T_new=4096, n_batches=4):
    global B, T, TH, NTB, NQ
    B = n_batches
    T = T_new
    TH = T // 2
    NTB = T // TB
    NQ = 8 * NTB
EPS = 1e-5
MASKVAL = -30000.0
BSAFE = -0.125 * 1.02

ORIG64 = list(range(0, 16)) + list(range(32, 48)) + list(range(16, 32)) + list(range(48, 64))
SHUF_MASK = list(range(16, 32)) + list(range(16))


class Plan:
    ENGS = ("pe", "act", "dve", "sync", "gp")
    SERIAL = {"act": "sACT", "dve": "sDVE", "gp": "sGP"}

    def __init__(self):
        self.ops = {e: [] for e in self.ENGS}
        self.cnt = {}
        self.last_wait = {}

    def wait(self, eng, sem, val):
        if not val:
            return
        key = (eng, sem)
        if self.last_wait.get(key, 0) >= val:
            return
        self.last_wait[key] = val
        self.ops[eng].append(("wait", sem, val))

    def wait_now(self, eng, sem):
        self.wait(eng, sem, self.cnt.get(sem, 0))

    def emit(self, eng, fn, inc=None, inc_n=1, ser=True):
        eng_sem = self.SERIAL.get(eng)
        if eng_sem and (inc is None or inc == eng_sem) and not (
                isinstance(inc, str) and inc.startswith("d")):
            if ser:
                self.wait(eng, eng_sem, self.cnt.get(eng_sem, 0))
            inc, inc_n = eng_sem, 1
        v = None
        if inc is not None:
            v = self.cnt.get(inc, 0) + inc_n
            self.cnt[inc] = v
        self.ops[eng].append(("op", fn, inc, inc_n))
        return v

    def snap(self):
        return dict(self.cnt)

    def barrier(self, snapd):
        for e in self.ENGS:
            for sem, v in snapd.items():
                self.wait(e, sem, v)

    def replay(self, eng, E, sems):
        for item in self.ops[eng]:
            if item[0] == "wait":
                E.wait_ge(sems[item[1]], item[2])
            else:
                ins = item[1](E)
                if item[2] is not None and ins is not None:
                    ins.then_inc(sems[item[2]], item[3])


def build_nc(debug=False, replica_groups=None):
    if replica_groups is None:
        replica_groups = [[2 * i, 2 * i + 1] for i in range(4)]
    nc = bass.Bass("TRN2", target_bir_lowering=False)

    x_full = nc.dram_tensor("x_full", [T, D], F32, kind="ExternalInput").ap()
    x_own = nc.dram_tensor("x_own", [TH, D], F32, kind="ExternalInput").ap()
    wq = nc.dram_tensor("wq", [D, 1024], F32, kind="ExternalInput").ap()
    wk = nc.dram_tensor("wk", [D, 512], F32, kind="ExternalInput").ap()
    wv = nc.dram_tensor("wv", [D, 256], F32, kind="ExternalInput").ap()
    wo = nc.dram_tensor("wo", [D, D], F32, kind="ExternalInput").ap()
    wff = nc.dram_tensor("wff", [D, D], F32, kind="ExternalInput").ap()
    bias_in = nc.dram_tensor("bias_t", [128, 16], F32, kind="ExternalInput").ap()
    crep = nc.dram_tensor("crep", [128, T], BF16, kind="ExternalInput").ap()
    srep = nc.dram_tensor("srep", [128, T], BF16, kind="ExternalInput").ap()
    maskt = nc.dram_tensor("maskt", [128, 128], F32, kind="ExternalInput").ap()
    identf = nc.dram_tensor("identf", [128, 128], F32, kind="ExternalInput").ap()
    consts = nc.dram_tensor("consts", [128, 8], F32, kind="ExternalInput").ap()
    coreoff = nc.dram_tensor("coreoff", [1, 2], I32, kind="ExternalInput").ap()
    ind2_in = nc.dram_tensor("ind2", [2, 128], F32, kind="ExternalInput").ap()

    out_own = nc.dram_tensor("out_own", [TH, D], F32, kind="ExternalOutput").ap()
    wo_bf = nc.dram_tensor("wo_bf", [D, D], BF16).ap()
    sgnT = nc.dram_tensor("sgnT", [D, D], BF16).ap()
    q_dram = nc.dram_tensor("q_dram", [128, 8 * T], BF16).ap()
    CCR = D + 32
    cc_in = nc.dram_tensor("cc_in", [CCR, TH], BF16).ap()
    cc_out = nc.dram_tensor("cc_out", [2 * CCR, TH], BF16).ap()
    dbg = {}
    if debug:
        for nm, sh in [("dbg_k", [128, 4 * T]),
                       ("dbg_v", [128, (T // 128) * 260]),
                       ("dbg_cc", [CCR, TH]), ("dbg_l", [128, 512]),
                       ("dbg_q", [128, 8 * T])]:
            dbg[nm] = nc.dram_tensor(nm, sh, BF16, kind="ExternalOutput").ap()
        dbg["dbg_negB"] = nc.dram_tensor("dbg_negB", [128, NQ], F32, kind="ExternalOutput").ap()
        dbg["dbg_o1"] = nc.dram_tensor("dbg_o1", [128, NKC * 512], BF16, kind="ExternalOutput").ap()
        dbg["dbg_x1"] = nc.dram_tensor("dbg_x1", [128, 4 * D], F32, kind="ExternalOutput").ap()
        dbg["dbg_y"] = nc.dram_tensor("dbg_y", [128, NKC * 512], BF16, kind="ExternalOutput").ap()

    st_root = contextlib.ExitStack()
    with st_root:
        def sb(stk, name, shape, dtype):
            return stk.enter_context(nc.sbuf_tensor(name, shape, dtype))

        def psm(stk, name, shape, dtype=F32):
            return stk.enter_context(nc.psum_tensor(name, shape, dtype))

        sems = {n: st_root.enter_context(nc.semaphore(n))
                for n in ["sPE", "sACT", "sDVE", "sGP", "sCC",
                          "dInit", "dIdent", "dWq", "dWk", "dWv", "dWoCast", "dRope0", "dRope1",
                          "dXA0", "dXA1", "dXA2", "dXA3", "dQS0", "dQS1", "dWff", "dSgT",
                          "dQL0", "dQL1", "dCC", "dCtx", "dWO", "dXC", "dOut", "dWff0", "dWff1", "dSgT2", "dSgT1", "dSgT3",
                          "dDbg0", "dDbg1", "dDbg2", "dDbg3", "dDbg4"]}

        c_ident_bf = sb(st_root, "c_ident_bf", [128, 128], BF16)
        c_ident_f = sb(st_root, "c_ident_f", [128, 128], F32)
        c_mask = sb(st_root, "c_mask", [128, 128], F32)
        c_const = sb(st_root, "c_const", [128, 8], F32)
        c_bias = sb(st_root, "c_bias", [128, 16], F32)
        c_alpha = sb(st_root, "c_alpha", [128, 16], F32)
        c_ones_bf = sb(st_root, "c_ones_bf", [128, 1], BF16)
        c_ones_f = sb(st_root, "c_ones_f", [1, 128], F32)
        c_Bq = sb(st_root, "c_Bq", [1, NQ], F32)
        c_kn2 = sb(st_root, "c_kn2", [1, 8], F32)
        c_negB = sb(st_root, "c_negB", [128, NQ], F32)
        c_off = sb(st_root, "c_off", [1, 2], I32)
        c_ind2 = sb(st_root, "c_ind2", [2, 128], BF16)
        KN = NQ + 8
        TMP = NQ + 16
        c_tiny = sb(st_root, "c_tiny", [1, NQ + 24], F32)

        nc.const_aps.register(F32, EPS, c_const[:, 0:1])
        nc.const_aps.register(F32, -0.5, c_const[:, 1:2])
        nc.const_aps.register(F32, 0.125, c_const[:, 2:3])
        nc.const_aps.register(F32, 1.0 / D, c_const[:, 3:4])
        nc.const_aps.register(F32, -1.0, c_const[:, 4:5])
        nc.const_aps.register(F32, 0.5, c_const[:, 5:6])

        p = Plan()
        W, E, WN = p.wait, p.emit, p.wait_now

        for dst, src in [(c_mask, maskt), (c_const, consts), (c_bias, bias_in),
                         (c_ident_f, identf), (c_off, coreoff)]:
            d_init = E("sync", lambda e, d=dst, s=src: e.dma_start(out=d[:, :], in_=s),
                       inc="dInit", inc_n=16)
        g_ident = E("gp", lambda e: e.dma_start(out=c_ident_bf[:, :], in_=identf),
                    inc="dIdent", inc_n=16)
        for tile, val in [(c_ones_bf, 1.0), (c_ones_f, 1.0), (c_Bq, 0.0), (c_kn2, 0.0)]:
            g_ms = E("gp", lambda e, t=tile, v=val: e.memset(t[:, :], v), inc="sGP")
        g_ind2 = E("gp", lambda e: e.dma_start(out=c_ind2[:, :], in_=ind2_in),
                   inc="dIdent", inc_n=16)
        g_ident = g_ind2
        W("act", "dInit", d_init)
        W("dve", "dInit", d_init)
        W("dve", "sGP", g_ms)
        W("pe", "dInit", d_init)
        W("pe", "sGP", g_ms)

        stAB = st_root.enter_context(contextlib.ExitStack())
        k_dup = sb(stAB, "k_dup", [128, 4 * T], BF16)
        v_tok = sb(stAB, "v_tok", [128, (T // 128) * 260], BF16)
        g_msv = E("gp", lambda e: e.memset(v_tok[:, :], 1.0), inc="sGP")

        stA = st_root.enter_context(contextlib.ExitStack())
        wq_bf = sb(stA, "wq_bf", [128, NKC * 1024], BF16)
        wk_bf = sb(stA, "wk_bf", [128, NKC * 512], BF16)
        wv_bf = sb(stA, "wv_bf", [128, NKC * 256], BF16)
        lnxT = sb(stA, "lnxT", [128, NKC * TB], BF16)
        x_pool = [sb(stA, f"x_pool{i}", [128, D], F32) for i in range(4)]
        lnx_pool = [sb(stA, f"lnx_pool{i}", [128, D], BF16) for i in range(4)]
        stats = sb(stA, "stats", [128, 32], F32)
        rope_c = [sb(stA, f"rope_c{i}", [128, TB], BF16) for i in range(2)]
        rope_s = [sb(stA, f"rope_s{i}", [128, TB], BF16) for i in range(2)]
        qtmp = [sb(stA, f"qtmp{i}", [128, TB], BF16) for i in range(2)]
        qsw = [sb(stA, f"qsw{i}", [128, TB], BF16) for i in range(2)]
        qsq_bf = [sb(stA, f"qsq_bf{i}", [128, TB], BF16) for i in range(2)]
        q_out = [sb(stA, f"q_out{i}", [128, TB], BF16) for i in range(2)]

        ps_tp = [psm(stA, f"ps_tp{i}", [128, 512], BF16) for i in range(2)]
        ps_proj = [psm(stA, f"ps_proj{i}", [128, 512], F32) for i in range(2)]
        ps_v = [psm(stA, f"ps_v{i}", [128, 256], F32) for i in range(2)]
        ps_st = psm(stA, "ps_st", [128, 1024], F32)

        for kc in range(NKC):
            g_wq = E("gp", lambda e, kc=kc: e.dma_start(
                out=wq_bf[:, kc * 1024:(kc + 1) * 1024], in_=wq[kc * 128:(kc + 1) * 128, :]),
                inc="dWq", inc_n=16)
        for kc in range(NKC):
            g_wk = E("gp", lambda e, kc=kc: e.dma_start(
                out=wk_bf[:, kc * 512:(kc + 1) * 512], in_=wk[kc * 128:(kc + 1) * 128, :]),
                inc="dWk", inc_n=16)
        for kc in range(NKC):
            g_wv = E("gp", lambda e, kc=kc: e.dma_start(
                out=wv_bf[:, kc * 256:(kc + 1) * 256], in_=wv[kc * 128:(kc + 1) * 128, :]),
                inc="dWv", inc_n=16)
        for kc in range(NKC):
            E("gp", lambda e, kc=kc: e.dma_start(
                out=wo_bf[kc * 128:(kc + 1) * 128, :], in_=wo[kc * 128:(kc + 1) * 128, :]),
                inc="dWoCast", inc_n=16)

        def stat_c(tile, sg, i):
            o = (sg % 4) * 8 + i
            return tile[:, o:o + 1]

        def ln_stats(tile_src, sg, stattile, dumptile, war=0):
            W("act", "sDVE", war)
            E("dve", lambda e: e.reduce_sum(stat_c(stattile, sg, 0), tile_src, AX))
            E("dve", lambda e: e.tensor_scalar(
                stat_c(stattile, sg, 1), stat_c(stattile, sg, 0), 1.0 / D, None, ALU.mult))
            a_s = E("act", lambda e: e.activation(
                dumptile[:, :], tile_src, AF.Square, accum_out=stat_c(stattile, sg, 2)),
                inc="sACT")
            W("dve", "sACT", a_s)
            E("dve", lambda e: e.tensor_scalar(
                stat_c(stattile, sg, 3), stat_c(stattile, sg, 2), 1.0 / D, None, ALU.mult))
            E("dve", lambda e: e.tensor_scalar(
                stat_c(stattile, sg, 4), stat_c(stattile, sg, 1), stat_c(stattile, sg, 1),
                None, ALU.mult))
            d_v = E("dve", lambda e: e.tensor_sub(
                stat_c(stattile, sg, 5), stat_c(stattile, sg, 3), stat_c(stattile, sg, 4)),
                inc="sDVE")
            W("act", "sDVE", d_v)
            E("act", lambda e: e.activation(
                stat_c(stattile, sg, 6), stat_c(stattile, sg, 5), AF.Ln, bias=EPS))
            a_r = E("act", lambda e: e.activation(
                stat_c(stattile, sg, 6), stat_c(stattile, sg, 6), AF.Exp, scale=-0.5),
                inc="sACT")
            W("dve", "sACT", a_r)
            d_n = E("dve", lambda e: e.tensor_scalar(
                stat_c(stattile, sg, 7), stat_c(stattile, sg, 1), stat_c(stattile, sg, 6),
                -1.0, ALU.mult, ALU.mult), inc="sDVE")
            return stat_c(stattile, sg, 6), stat_c(stattile, sg, 7)

        apply_done = {}
        tp_hist = [0, 0]

        def tp_block(src_fn, dst4_fn, banks, ident, n):
            ng = (n + 3) // 4
            for g in range(ng):
                bank = g % 2
                W("pe", "sACT", tp_hist[bank])
                for sub in range(4):
                    kc = g * 4 + sub
                    if kc >= n:
                        break
                    t_tp = E("pe", lambda e, kc=kc, sub=sub, bank=bank: e.transpose(
                        banks[bank][:, 128 * sub:128 * (sub + 1)], src_fn(kc),
                        ident[:, :]),
                        inc=("sPE" if (sub == 3 or kc == n - 1) else None))
                W("act", "sPE", t_tp)
                tp_hist[bank] = E("act", lambda e, g=g, bank=bank: dst4_fn(e, g, banks[bank]),
                                  inc="sACT", ser=False)
            return tp_hist[(ng - 1) % 2]

        proj_evac = {}
        v_evac = {}
        q_store = {}
        qsq_done = {}
        q_idx = 0
        sgn_prev = None
        sgn_hist = {}
        wprep_tp = [0]
        last_rope_dve = 0
        v_last_mm = 0
        pe_n_hist = {}
        pend_norm = [None]
        d_stt_hist = {}
        norm_idx = [0]
        g_rot = 0
        rope_done = {}

        rope_last = {}
        for tb in range(NTB):
            tpar = tb % 2
            W("sync", "sDVE", rope_last.get(tb - 2, 0))
            E("sync", lambda e, tb=tb, tpar=tpar: e.dma_start(
                out=rope_c[tpar][:, :], in_=crep[:, tb * TB:(tb + 1) * TB]),
                inc=f"dRope{tpar}", inc_n=16)
            d_rope = E("sync", lambda e, tb=tb, tpar=tpar: e.dma_start(
                out=rope_s[tpar][:, :], in_=srep[:, tb * TB:(tb + 1) * TB]),
                inc=f"dRope{tpar}", inc_n=16)

            for st4 in range(4):
                sg = tb * 4 + st4
                slot = sg % 4
                tok0 = sg * 128
                W("sync", "sDVE", apply_done.get(sg - 4, 0))
                d_x = E("sync", lambda e, slot=slot, tok0=tok0: e.dma_start(
                    out=x_pool[slot][:, :], in_=x_full[tok0:tok0 + 128, :]),
                    inc=f"dXA{slot}", inc_n=16)
                W("dve", f"dXA{slot}", d_x)
                W("act", f"dXA{slot}", d_x)
                rstd, negmr = ln_stats(x_pool[slot][:, :], sg, stats, lnx_pool[slot],
                                       war=apply_done.get(sg - 4, 0))
                apply_done[sg] = E("dve", lambda e, slot=slot, rstd=rstd, negmr=negmr: e.tensor_scalar(
                    lnx_pool[slot][:, :], x_pool[slot][:, :], rstd, negmr,
                    ALU.mult, ALU.add), inc="sDVE")
                W("pe", "dIdent", g_ident)
                W("pe", "sDVE", apply_done[sg])
                if st4 == 0:
                    W("act", "sPE", v_last_mm)
                tp_block(lambda kc, slot=slot: lnx_pool[slot][:, 128 * kc:128 * (kc + 1)],
                         lambda e, g, bank, st4=st4: e.copy(
                             lnxT[:, :].rearrange("p (c t) -> p c t", c=NKC)
                             [:, 4 * g:4 * g + 4, st4 * 128:(st4 + 1) * 128],
                             bank[:, :].rearrange("p (c q) -> p c q", c=4)),
                         ps_tp, c_ident_bf, NKC)
            lnxT_ready = tp_hist[(NKC - 1) % 2]

            W("pe", "sACT", lnxT_ready)
            rot = 0

            def emit_norm():
                if pend_norm[0] is None:
                    return
                rp_, asq_, kind_, m_, tb_ = pend_norm[0]
                pend_norm[0] = None
                np_ = norm_idx[0] % 2
                norm_idx[0] += 1
                W("pe", "sACT", asq_)
                W("pe", "sDVE", d_stt_hist.get(np_, 0))
                pe_n = E("pe", lambda e, rp_=rp_, np_=np_: e.matmul(
                    ps_st[0:1, 512 * np_:512 * np_ + TB], c_ones_bf[:, :],
                    qsq_bf[rp_][:, :], start=True, stop=True), inc="sPE")
                pe_n_hist[rp_] = pe_n
                W("dve", "sPE", pe_n)
                if kind_ == "q":
                    d_stt_hist[np_] = E("dve", lambda e, m_=m_, tb_=tb_, np_=np_: e.reduce_max(
                        c_Bq[0:1, m_ * NTB + tb_: m_ * NTB + tb_ + 1],
                        ps_st[0:1, 512 * np_:512 * np_ + TB], AX), inc="sDVE")
                else:
                    E("dve", lambda e, np_=np_: e.reduce_max(
                        c_tiny[0:1, TMP:TMP + 1], ps_st[0:1, 512 * np_:512 * np_ + TB], AX))
                    d_stt_hist[np_] = E("dve", lambda e, m_=m_: e.tensor_max(
                        c_kn2[0:1, m_:m_ + 1], c_kn2[0:1, m_:m_ + 1],
                        c_tiny[0:1, TMP:TMP + 1]), inc="sDVE")

            for kind, nchunk in (("q", 8), ("k", 4)):
                for m in range(nchunk):
                    bank = rot % 2
                    W("pe", "dWq" if kind == "q" else "dWk",
                      g_wq if kind == "q" else g_wk)
                    if proj_evac.get(rot - 2):
                        W("pe", "sACT", proj_evac[rot - 2])
                    for kc in range(NKC):
                        if kind == "q":
                            lhsT = wq_bf[:, kc * 1024 + m * 128: kc * 1024 + (m + 1) * 128]
                        else:
                            lhsT = wk_bf[:, kc * 512 + m * 128: kc * 512 + (m + 1) * 128]
                        pe_mm = E("pe", lambda e, lhsT=lhsT, kc=kc, bank=bank: e.matmul(
                            ps_proj[bank][:, :], lhsT, lnxT[:, kc * TB:(kc + 1) * TB],
                            start=(kc == 0), stop=(kc == NKC - 1)),
                            inc=("sPE" if kc == NKC - 1 else None))
                    emit_norm()
                    rp = g_rot % 2
                    W("act", "sPE", pe_mm)
                    W("act", "sDVE", rope_done.get(g_rot - 2, 0))
                    a_cp = E("act", lambda e, rp=rp, bank=bank: e.copy(
                        qtmp[rp][:, :], ps_proj[bank][:, :]), inc="sACT", ser=False)
                    proj_evac[rot] = a_cp
                    W("dve", "sACT", a_cp)
                    W("dve", f"dRope{tpar}", d_rope)
                    if kind == "q":
                        W("dve", f"dQS{q_idx % 2}", q_store.get(q_idx - 2, 0))
                        W("dve", "sACT", qsq_done.get(rot - 2, 0))
                        dest = q_out[q_idx % 2][:, :]
                    else:
                        dest = k_dup[:, m * T + tb * TB: m * T + (tb + 1) * TB]
                    E("dve", lambda e, rp=rp: e.stream_shuffle(
                        qsw[rp][:, :], qtmp[rp][:, :], SHUF_MASK), ser=False)
                    E("dve", lambda e, rp=rp, tpar=tpar: e.tensor_mul(
                        qtmp[rp][:, :], qtmp[rp][:, :], rope_c[tpar][:, :]), ser=False)
                    E("dve", lambda e, rp=rp, tpar=tpar: e.tensor_mul(
                        qsw[rp][:, :], qsw[rp][:, :], rope_s[tpar][:, :]))
                    d_rp = E("dve", lambda e, rp=rp, dest=dest: e.tensor_add(
                        dest, qtmp[rp][:, :], qsw[rp][:, :]), inc="sDVE")
                    rope_done[g_rot] = d_rp
                    g_rot += 1
                    last_rope_dve = d_rp
                    if kind == "q":
                        W("sync", "sDVE", d_rp)
                        q_store[q_idx] = E("sync", lambda e, m=m, tb=tb, dest=dest: e.dma_start(
                            out=q_dram[:, m * T + tb * TB: m * T + (tb + 1) * TB], in_=dest),
                            inc=f"dQS{q_idx % 2}", inc_n=16)
                        q_idx += 1
                    W("act", "sDVE", d_rp)
                    W("act", "sPE", pe_n_hist.get(rp, 0))
                    a_sq2 = E("act", lambda e, rp=rp, dest=dest: e.activation(
                        qsq_bf[rp][:, :], dest, AF.Square), inc="sACT", ser=False)
                    qsq_done[rot] = a_sq2
                    pend_norm[0] = (rp, a_sq2, kind, m, tb)
                    rot += 1

            emit_norm()
            rope_last[tb] = last_rope_dve
            W("pe", "dWv", g_wv)
            for tt in range(4):
                bank = tt % 2
                if v_evac.get(tt - 2):
                    W("pe", "sACT", v_evac[tt - 2])
                for kc in range(NKC):
                    pe_v = E("pe", lambda e, kc=kc, tt=tt, bank=bank: e.matmul(
                        ps_v[bank][:, :],
                        lnxT[:, kc * TB + tt * 128: kc * TB + (tt + 1) * 128],
                        wv_bf[:, kc * 256:(kc + 1) * 256],
                        start=(kc == 0), stop=(kc == NKC - 1)),
                        inc=("sPE" if kc == NKC - 1 else None))
                W("act", "sPE", pe_v)
                W("act", "sGP", g_msv)
                v_evac[tt] = E("act", lambda e, tb=tb, tt=tt, bank=bank: e.copy(
                    v_tok[:, (tb * 4 + tt) * 260:(tb * 4 + tt + 1) * 260]
                    .rearrange("p (i c) -> p i c", i=4)[:, :, 0:64],
                    ps_v[bank][:, :].rearrange("p (i c) -> p i c", i=4)),
                    inc="sACT", ser=False)
            v_last_mm = pe_v


        WN("act", "sDVE")
        E("act", lambda e: e.activation(c_tiny[0:1, 0:NQ], c_Bq[0:1, :], AF.Ln))
        E("act", lambda e: e.activation(c_tiny[0:1, 0:NQ], c_tiny[0:1, 0:NQ], AF.Exp, scale=0.5))
        E("act", lambda e: e.activation(c_tiny[0:1, KN:KN + 4], c_kn2[0:1, 0:4], AF.Ln))
        a_kn = E("act", lambda e: e.activation(
            c_tiny[0:1, KN:KN + 4], c_tiny[0:1, KN:KN + 4], AF.Exp, scale=0.5), inc="sACT")
        W("dve", "sACT", a_kn)
        for m in range(8):
            E("dve", lambda e, m=m: e.tensor_mul(
                c_tiny[0:1, m * NTB:(m + 1) * NTB], c_tiny[0:1, m * NTB:(m + 1) * NTB],
                c_tiny[0:1, KN + m // 2:KN + 1 + m // 2].broadcast_to([1, NTB])))
        d_nb = E("dve", lambda e: e.tensor_scalar(
            c_tiny[0:1, 0:NQ], c_tiny[0:1, 0:NQ], BSAFE, None, ALU.mult), inc="sDVE")
        W("pe", "sDVE", d_nb)
        pe_nb = E("pe", lambda e: e.matmul(
            ps_st[:, 0:NQ], c_ones_f[0:1, :], c_tiny[0:1, 0:NQ],
            start=True, stop=True), inc="sPE")
        W("act", "sPE", pe_nb)
        a_nb = E("act", lambda e: e.copy(c_negB[:, :], ps_st[:, 0:NQ]), inc="sACT")

        a_end = p.snap()
        if debug:
            p.barrier(a_end)
            for nm, src in (("dbg_k", k_dup), ("dbg_v", v_tok)):
                E("sync", lambda e, nm=nm, src=src: e.dma_start(out=dbg[nm], in_=src[:, :]),
                  inc="dDbg0", inc_n=16)
            E("sync", lambda e: e.dma_start(out=dbg["dbg_negB"], in_=c_negB[:, :]),
              inc="dDbg0", inc_n=16)
            p.barrier(p.snap())

        stA.close()
        stB = st_root.enter_context(contextlib.ExitStack())
        q_slab = [sb(stB, f"q_slab{i}", [128, TB], BF16) for i in range(2)]
        p_buf4 = sb(stB, "p_buf4", [128, 4096], BF16)
        l_acc = sb(stB, "l_acc", [128, 512], BF16)
        rl_acc = sb(stB, "rl_acc", [128, 512], F32)
        ctx_st = [sb(stB, f"ctx_st{i}", [128, 1024], BF16) for i in range(2)]
        ps_s4 = psm(stB, "ps_s4", [128, 2048], F32)
        ps_ctx = psm(stB, "ps_ctx", [128, 1024], F32)
        ps_lh1 = psm(stB, "ps_lh1", [128, 1024], F32)

        p.barrier(a_end)
        W("pe", "sACT", a_nb)
        g_msl = E("gp", lambda e: e.memset(l_acc[:, :], 1.0), inc="sGP")
        W("gp", "sGP", g_msl)

        prev_exp_last = 0
        prev_av_last = 0
        prev_tail = 0
        prev_evac = 0
        q_slab_free = {}
        it_b = 0
        for qc in range(NTB):
            half = (qc * 512) // TH
            col0 = (qc * 512) % TH
            for ch in range(8):
                gl = ch // 2
                col = ch * NTB + qc
                jmax = 4 * qc + 4
                qs = it_b % 2
                par_i = it_b % 2
                W("sync", "sPE", q_slab_free.get(it_b - 2, 0))
                d_qs = E("sync", lambda e, qs=qs, ch=ch, qc=qc: e.dma_start(
                    out=q_slab[qs][:, :],
                    in_=q_dram[:, ch * T + 512 * qc: ch * T + 512 * (qc + 1)]),
                    inc=f"dQL{qs}", inc_n=16)
                sc_inc = {}
                sc_src = {}
                ex_inc = {}
                av_inc = {}

                def scores(j, qc=qc, ch=ch, gl=gl, qs=qs, d_qs=d_qs, sc_inc=None,
                           sc_src=None, ex_inc=None):
                    par = j % 2
                    dj = j - 4 * qc
                    qoff = 128 * dj if dj > 0 else 0
                    if ex_inc.get(j - 2):
                        W("pe", "sACT", ex_inc[j - 2])
                    elif j < 2:
                        W("pe", "sACT", prev_exp_last)
                    if j == 0:
                        W("pe", f"dQL{qs}", d_qs)
                    for h in range(2):
                        lo = 64 * h
                        v = E("pe", lambda e, j=j, par=par, qoff=qoff, lo=lo, h=h, qs=qs: e.matmul(
                            ps_s4[:, 1024 * par + 512 * h + qoff: 1024 * par + 512 * h + 512],
                            k_dup[lo:lo + 64, gl * T + 128 * j: gl * T + 128 * (j + 1)],
                            q_slab[qs][lo:lo + 64, qoff:512],
                            start=True, stop=True),
                            inc=("sPE" if h == 1 else None))
                    sc_inc[j], sc_src[j] = v, "sPE"
                    if dj >= 0:
                        W("dve", "sPE", v)
                        mview = c_mask[:, :].unsqueeze(1).broadcast_to([128, 2, 128])
                        sview = ps_s4[:, 1024 * par:1024 * par + 1024].rearrange(
                            "p (b n) -> p b n", b=2)[:, :, qoff:qoff + 128]
                        sc_inc[j] = E("dve", lambda e, sview=sview, mview=mview: e.tensor_tensor(
                            sview, sview, mview, ALU.add), inc="sDVE", ser=False)
                        sc_src[j] = "sDVE"

                def expj(j, qc=qc, col=col, sc_inc=None, sc_src=None, ex_inc=None, av_inc=None):
                    par = j % 2
                    dj = j - 4 * qc
                    qoff = 128 * dj if dj > 0 else 0
                    W("act", sc_src[j], sc_inc[j])
                    if av_inc.get(j - 4):
                        W("act", "sPE", av_inc[j - 4])
                    elif j < 4:
                        W("act", "sPE", prev_av_last)
                    sview = ps_s4[:, 1024 * par:1024 * par + 1024].rearrange(
                        "p (b n) -> p b n", b=2)[:, :, qoff:512]
                    pview = p_buf4[:, 1024 * (j % 4):1024 * (j % 4) + 1024].rearrange(
                        "p (b n) -> p b n", b=2)[:, :, qoff:512]
                    ex_inc[j] = E("act", lambda e, sview=sview, pview=pview, col=col: e.activation(
                        pview, sview, AF.Exp, bias=c_negB[:, col:col + 1], scale=0.125),
                        inc="sACT", ser=False)

                def expj_pair(j, col=col, sc_inc=None, sc_src=None, ex_inc=None,
                              av_inc=None):
                    W("act", sc_src[j], sc_inc[j])
                    W("act", sc_src[j + 1], sc_inc[j + 1])
                    if av_inc.get(j - 3):
                        W("act", "sPE", av_inc[j - 3])
                    elif j < 4:
                        W("act", "sPE", prev_av_last)
                    sview = ps_s4[:, :].rearrange("p (b n) -> p b n", b=4)
                    pview = p_buf4[:, 1024 * (j % 4):1024 * (j % 4) + 2048].rearrange(
                        "p (b n) -> p b n", b=4)
                    e_p = E("act", lambda e, sview=sview, pview=pview, col=col: e.activation(
                        pview, sview, AF.Exp, bias=c_negB[:, col:col + 1], scale=0.125),
                        inc="sACT", ser=False)
                    ex_inc[j] = e_p
                    ex_inc[j + 1] = e_p

                def av(j, qc=qc, gl=gl, jmax=jmax, par_i=par_i, prev_evac=prev_evac,
                       ex_inc=None, av_inc=None):
                    par = j % 2
                    dj = j - 4 * qc
                    qoff = 128 * dj if dj > 0 else 0
                    W("pe", "sACT", ex_inc[j])
                    if j == 0:
                        W("pe", "sDVE", prev_evac)
                    pb = 1024 * (j % 4)
                    E("pe", lambda e, j=j, qoff=qoff, gl=gl, pb=pb: e.matmul(
                        ps_ctx[0:65, qoff:512],
                        v_tok[:, j * 260 + 65 * gl: j * 260 + 65 * gl + 65],
                        p_buf4[:, pb + qoff:pb + 512],
                        start=(j == 0), stop=(j == jmax - 1),
                        tile_position=(0, 0)))
                    E("pe", lambda e, j=j, qoff=qoff, gl=gl, pb=pb: e.matmul(
                        ps_ctx[64:128, 512 + qoff:1024],
                        v_tok[:, j * 260 + 65 * gl: j * 260 + 65 * gl + 64],
                        p_buf4[:, pb + 512 + qoff:pb + 1024],
                        start=(j == 0), stop=(j == jmax - 1),
                        tile_position=(0, 64)))
                    av_inc[j] = E("pe", lambda e, j=j, qoff=qoff, par_i=par_i, pb=pb: e.matmul(
                        ps_lh1[0:1, 512 * par_i + qoff: 512 * par_i + 512],
                        c_ones_bf[:, :],
                        p_buf4[:, pb + 512 + qoff:pb + 1024],
                        start=(j == 0), stop=(j == jmax - 1),
                        tile_position=(0, 0)),
                        inc="sPE")

                kw = dict(sc_inc=sc_inc, sc_src=sc_src, ex_inc=ex_inc)
                scores(0, **kw)
                if jmax > 1:
                    scores(1, **kw)
                j = 0
                while j < jmax:
                    if j % 2 == 0 and j + 1 < 4 * qc:
                        expj_pair(j, av_inc=av_inc, **kw)
                        av(j, ex_inc=ex_inc, av_inc=av_inc)
                        if j + 2 < jmax:
                            scores(j + 2, **kw)
                        av(j + 1, ex_inc=ex_inc, av_inc=av_inc)
                        if j + 3 < jmax:
                            scores(j + 3, **kw)
                        j += 2
                    else:
                        expj(j, av_inc=av_inc, **kw)
                        av(j, ex_inc=ex_inc, av_inc=av_inc)
                        if j + 2 < jmax:
                            scores(j + 2, **kw)
                        j += 1

                prev_exp_last = ex_inc[jmax - 1]
                prev_av_last = av_inc[jmax - 1]
                q_slab_free[it_b] = sc_inc[jmax - 1] if sc_src[jmax - 1] == "sPE" else p.cnt["sPE"]
                it_b += 1
                st_t = ctx_st[par_i]
                W("dve", "sPE", av_inc[jmax - 1])
                W("dve", "dCC", prev_tail)
                E("dve", lambda e, st_t=st_t: e.tensor_scalar(
                    st_t[0:65, 0:512], ps_ctx[0:65, 0:512], 1.0, None, ALU.mult),
                    ser=False)
                E("dve", lambda e, st_t=st_t: e.tensor_scalar(
                    st_t[64:128, 512:1024], ps_ctx[64:128, 512:1024],
                    1.0, None, ALU.mult), ser=False)
                prev_evac = E("dve", lambda e, st_t=st_t, par_i=par_i: e.tensor_scalar(
                    st_t[0:1, 512:1024], ps_lh1[0:1, 512 * par_i:512 * par_i + 512],
                    1.0, None, ALU.mult), inc="sDVE", ser=False)
                W("gp", "sDVE", prev_evac)
                row0 = half * 1024 + 128 * ch
                E("gp", lambda e, st_t=st_t, row0=row0, col0=col0: e.dma_start(
                    out=cc_in[row0:row0 + 64, col0:col0 + 512],
                    in_=st_t[0:64, 0:512]), inc="dCC", inc_n=16)
                E("gp", lambda e, st_t=st_t, row0=row0, col0=col0: e.dma_start(
                    out=cc_in[row0 + 64:row0 + 128, col0:col0 + 512],
                    in_=st_t[64:128, 512:1024]), inc="dCC", inc_n=16)
                lrow = qc * 16 + 2 * ch
                E("gp", lambda e, st_t=st_t, lrow=lrow: e.dma_start(
                    out=l_acc[lrow:lrow + 1, :], in_=st_t[64:65, 0:512]),
                    inc="dCC", inc_n=16)
                prev_tail = E("gp", lambda e, st_t=st_t, lrow=lrow: e.dma_start(
                    out=l_acc[lrow + 1:lrow + 2, :], in_=st_t[0:1, 512:1024]),
                    inc="dCC", inc_n=16)

        b_end = p.snap()

        W("dve", "dCC", p.cnt["dCC"])
        d_rl = E("dve", lambda e: e.reciprocal(rl_acc[:, :], l_acc[:, :]), inc="sDVE")
        W("gp", "sDVE", d_rl)
        for qc in range(NTB):
            half = (qc * 512) // TH
            col0 = (qc * 512) % TH
            E("gp", lambda e, qc=qc, half=half, col0=col0: e.dma_start(
                out=cc_in[D + 16 * half: D + 16 * half + 16, col0:col0 + 512],
                in_=rl_acc[16 * qc:16 * qc + 16, :]), inc="dCC", inc_n=16)
        W("gp", "dCC", p.cnt["dCC"])
        if debug:
            W("sync", "dCC", p.cnt["dCC"])
            W("sync", "sDVE", d_rl)
            E("sync", lambda e: e.dma_start(out=dbg["dbg_cc"], in_=cc_in),
              inc="dDbg4", inc_n=16)
            E("sync", lambda e: e.dma_start(out=dbg["dbg_l"], in_=l_acc[:, :]),
              inc="dDbg4", inc_n=16)
            E("sync", lambda e: e.dma_start(out=dbg["dbg_q"], in_=q_dram),
              inc="dDbg4", inc_n=16)
            W("sync", "dDbg4", p.cnt["dDbg4"])
            W("gp", "dDbg4", p.cnt["dDbg4"])
        E("gp", lambda e: e.collective_compute(
            "AllGather", ALU.bypass,
            replica_groups=replica_groups,
            ins=[cc_in[:, :].opt()], outs=[cc_out[:, :].opt()]), inc="sCC")

        W("sync", "dCC", p.cnt["dCC"])
        W("gp", "dCC", p.cnt["dCC"])
        stB.close()
        stAB.close()
        b_end.pop("dCC", None)
        p.barrier(b_end)

        stC = st_root.enter_context(contextlib.ExitStack())
        ctx_qc = sb(stC, "ctx_qc", [128, NKC * 512], BF16)
        wobuf = sb(stC, "wobuf", [128, 8 * D], BF16)
        big_ft = sb(stC, "big_ft", [128, NKC * 512], BF16)
        big_fy = sb(stC, "big_fy", [128, NKC * 512], BF16)
        x1b = sb(stC, "x1b", [128, 4 * D], F32)
        lnx1T = sb(stC, "lnx1T", [128, NKC * 512], BF16)
        x_t = sb(stC, "x_t", [128, D], F32)
        o1_t = [sb(stC, f"o1_t{i}", [128, D], BF16) for i in range(2)]
        y_t = [sb(stC, f"y_t{i}", [128, D], BF16) for i in range(2)]
        out_t = sb(stC, "out_t", [128, D], F32)
        lnx1_bf = [sb(stC, f"lnx1_bf{i}", [128, D], BF16) for i in range(2)]
        stats2 = sb(stC, "stats2", [128, 32], F32)
        sq_dump2 = sb(stC, "sq_dump2", [128, D], BF16)
        rl2h = sb(stC, "rl2h", [2, 8 * 512], BF16)
        wprep0 = sb(stC, "wprep0", [128, D], F32)
        wprep = [wprep0, None]
        wprep_bf = sb(stC, "wprep_bf", [128, D], BF16)
        sgnT_sb = [sb(stC, f"sgnT_sb{i}", [128, D], BF16) for i in range(2)]
        alpha_raw = sb(stC, "alpha_raw", [128, 16], F32)
        rl_full = [sb(stC, f"rl_full{i}", [128, 512], F32) for i in range(2)]
        ps_mm = [psm(stC, f"ps_mm{i}", [128, 512], F32) for i in range(2)]
        ps_tc = [psm(stC, f"ps_tc{i}", [128, 512], BF16) for i in range(2)]
        ps_rl = [psm(stC, f"ps_rl{i}", [128, 512], F32) for i in range(2)]

        W("gp", "dInit", d_init)
        gp_off = [None]
        gp_off2 = [None]
        def _load_off(e):
            v = e.value_load(c_off[0:1, 0:1])
            gp_off[0] = e.snap(v, min_val=0, max_val=1024 * TH)
            v2 = e.value_load(c_off[0:1, 1:2])
            gp_off2[0] = e.snap(v2, min_val=0, max_val=16 * TH)
            return None
        p.ops["gp"].append(("op", _load_off, None, 1))

        wprep[1] = out_t
        sgn_store = {}
        sgn_hist = {}
        wprep_tp = [0]
        for oc in range(16):
            wp = oc % 2
            if sgn_hist.get(wp):
                W("sync", "sDVE", sgn_hist[wp])
            d_w = E("sync", lambda e, oc=oc, wp=wp: e.dma_start(
                out=wprep[wp][:, :], in_=wff[oc * 128:(oc + 1) * 128, :]),
                inc=f"dWff{wp}", inc_n=16)
            W("dve", f"dWff{wp}", d_w)
            W("dve", "sPE", wprep_tp[0])
            E("dve", lambda e, wp=wp: e.tensor_scalar(
                wprep_bf[:, :], wprep[wp][:, :], 0.0, 2.0, ALU.is_ge, ALU.mult))
            d_sg = E("dve", lambda e: e.tensor_scalar(
                wprep_bf[:, :], wprep_bf[:, :], 1.0, None, ALU.subtract), inc="sDVE")
            d_al = E("dve", lambda e, oc=oc, wp=wp: e.reduce_sum(
                alpha_raw[:, oc:oc + 1], wprep[wp][:, :], AX, apply_absolute_value=True),
                inc="sDVE")
            sgn_hist[wp] = d_al
            W("pe", "sDVE", d_sg)
            if sgn_store.get(wp):
                W("pe", f"dSgT{wp}" if wp else "dSgT", sgn_store[wp][0])
                W("pe", f"dSgT{2 + wp}", sgn_store[wp][1])
            tp_block(lambda kc, wp=wp: wprep_bf[:, 128 * kc:128 * (kc + 1)],
                     lambda e, g, bank, wp=wp: e.copy(
                         sgnT_sb[wp][:, 512 * g:512 * (g + 1)], bank[:, :]),
                     ps_tc, c_ident_bf, NKC)
            wprep_tp[0] = p.cnt["sPE"]
            tpl = tp_hist[((NKC + 3) // 4 - 1) % 2]
            W("act", "sACT", tpl)
            W("gp", "sACT", tpl)
            s0 = E("act", lambda e, oc=oc, wp=wp: e.dma_start(
                out=bass.AP(sgnT.tensor, 128 * oc, [[D, 128], [128 * D, 8], [1, 128]]),
                in_=sgnT_sb[wp][:, 0:1024].rearrange("p (c q) -> p c q", c=8)),
                inc=("dSgT1" if wp else "dSgT"), inc_n=16)
            s1 = E("gp", lambda e, oc=oc, wp=wp: e.dma_start(
                out=bass.AP(sgnT.tensor, 128 * oc + 8 * 128 * D,
                            [[D, 128], [128 * D, 8], [1, 128]]),
                in_=sgnT_sb[wp][:, 1024:2048].rearrange("p (c q) -> p c q", c=8)),
                inc=f"dSgT{2 + wp}", inc_n=16)
            sgn_store[wp] = (s0, s1)
        d_alpha = E("dve", lambda e: e.tensor_scalar(
            c_alpha[:, :], alpha_raw[:, :], 1.0 / D, None, ALU.mult), inc="sDVE")

        wo_mm_last = 0
        wobuf_free = 0
        dense_prev = []
        ffn_tp_last = 0
        sgC = [0]
        lnC_war = {}
        o1_war = {}
        y_war = {}
        lnx1_tp = {}
        out_store = 0
        x1_add_done = 0

        def dense(weight_dram, moving, affine, g_ctx_wait=0, moving_ready=None,
                  out_buf=None, halves=(0, 1), evac_d=None, close=True,
                  init_last=0):
            nonlocal wobuf_free
            last = init_last
            if evac_d is None:
                evac_d = {}
            for half in halves:
                W("sync", "sPE", wobuf_free if half == 0 else last)
                slabs = 0
                for k in range(8):
                    slabs = E("sync", lambda e, k=k, half=half, wd=weight_dram: e.dma_start(
                        out=wobuf[:, k * D:(k + 1) * D],
                        in_=wd[(half * 8 + k) * 128:(half * 8 + k + 1) * 128, :]),
                        inc="dWO", inc_n=16)
                for m in range(NKC):
                    bank = m % 2
                    if evac_d.get((half, m - 2)):
                        W("pe", evac_d[(half, m - 2)][0], evac_d[(half, m - 2)][1])
                    elif half == 1 and evac_d.get((0, NKC - 2 + m)):
                        W("pe", evac_d[(0, NKC - 2 + m)][0], evac_d[(0, NKC - 2 + m)][1])
                    elif half == 0 and dense_prev:
                        W("pe", dense_prev[-1][0], dense_prev[-1][1])
                    W("pe", "dWO", slabs)
                    if g_ctx_wait:
                        W("pe", "dCtx", g_ctx_wait)
                    if moving_ready is not None and m == 0:
                        W("pe", "sDVE", moving_ready[half * 8 + 7])
                    for k in range(8):
                        kc2 = half * 8 + k
                        mmv = E("pe", lambda e, k=k, m=m, bank=bank, kc2=kc2: e.matmul(
                            ps_mm[bank][:, :],
                            wobuf[:, k * D + m * 128: k * D + (m + 1) * 128],
                            moving[:, kc2 * 512:(kc2 + 1) * 512],
                            start=(k == 0), stop=(k == 7)),
                            inc=("sPE" if k == 7 else None))
                    last = mmv
                    if half == 0:
                        W("act", "sPE", mmv)
                        ev = E("act", lambda e, m=m, bank=bank, ob=out_buf: e.copy(
                            ob[:, m * 512:(m + 1) * 512], ps_mm[bank][:, :]),
                            inc="sACT", ser=False)
                        evac_d[(half, m)] = ("sACT", ev)
                    else:
                        W("dve", "sPE", mmv)
                        ev = E("dve", lambda e, m=m, bank=bank, ob=out_buf: e.tensor_tensor(
                            ob[:, m * 512:(m + 1) * 512],
                            ob[:, m * 512:(m + 1) * 512],
                            ps_mm[bank][:, :], ALU.add),
                            inc=(None if affine else "sDVE"))
                        if affine:
                            ev = E("dve", lambda e, m=m, ob=out_buf: e.tensor_scalar(
                                ob[:, m * 512:(m + 1) * 512],
                                ob[:, m * 512:(m + 1) * 512],
                                c_alpha[:, m:m + 1], c_bias[:, m:m + 1],
                                ALU.mult, ALU.add), inc="sDVE")
                        evac_d[(half, m)] = ("sDVE", ev)
            wobuf_free = last
            if close:
                dense_prev.append(evac_d[(1, NKC - 1)])
                return last, evac_d[(1, NKC - 1)]
            return last, evac_d

        mul_done_prev = [None]
        rl_ev_prev = [None]
        norm_state = {}
        pe_rl_box = [0]
        g_ctx_box = [0]
        o1_tp_last = [0]
        wo_state = {}
        NQI = TH // 512

        def ctx_loads(qi):
            W("gp", "sCC", 1)
            W("gp", "sPE", wo_state.get(("mm", qi - 1), 0))
            for hi in range(2):
                g_ctx_box[0] = E("gp", lambda e, qi=qi, hi=hi: e.dma_start(
                    out=bass.AP(ctx_qc, hi * 8 * 512,
                                [[NKC * 512, 128], [512, 8], [1, 512]]),
                    in_=bass.AP(cc_out.tensor, gp_off[0] + CCR * TH * hi + 512 * qi,
                                [[TH, 128], [128 * TH, 8], [1, 512]])),
                    inc="dCtx", inc_n=16)

        def norm_chunks(qi, c2s):
            st = norm_state.setdefault(qi, {"mul": {}, "rl": {}})
            for c2 in c2s:
                rpar = c2 % 2
                hi = c2 // 8
                if c2 % 8 == 0:
                    if hi == 1:
                        W("gp", "sPE", pe_rl_box[0])
                    g_ctx_box[0] = E("gp", lambda e, qi=qi, hi=hi: e.dma_start(
                        out=rl2h[0:2, :].rearrange("p (c q) -> p c q", c=8),
                        in_=bass.AP(cc_out.tensor,
                                    gp_off2[0] + CCR * TH * hi + D * TH + 512 * qi,
                                    [[TH, 2], [2 * TH, 8], [1, 512]])),
                        inc="dCtx", inc_n=16)
                W("pe", "dCtx", g_ctx_box[0])
                W("pe", "dIdent", g_ind2)
                if c2 >= 2:
                    W("pe", "sACT", st["rl"][c2 - 2])
                elif rl_ev_prev[0] is not None:
                    W("pe", "sACT", rl_ev_prev[0][14 + c2])
                pe_rl_box[0] = E("pe", lambda e, c2=c2, rpar=rpar: e.matmul(
                    ps_rl[rpar][:, :], c_ind2[:, :],
                    rl2h[0:2, (c2 % 8) * 512:(c2 % 8 + 1) * 512],
                    start=True, stop=True), inc="sPE")
                W("act", "sPE", pe_rl_box[0])
                if c2 >= 2:
                    W("act", "sDVE", st["mul"][c2 - 2])
                elif mul_done_prev[0] is not None:
                    W("act", "sDVE", mul_done_prev[0][14 + c2])
                st["rl"][c2] = E("act", lambda e, rpar=rpar: e.copy(
                    rl_full[rpar][:, :], ps_rl[rpar][:, :]), inc="sACT", ser=False)
                W("dve", "sACT", st["rl"][c2])
                st["mul"][c2] = E("dve", lambda e, c2=c2, rpar=rpar: e.tensor_mul(
                    ctx_qc[:, c2 * 512:(c2 + 1) * 512],
                    ctx_qc[:, c2 * 512:(c2 + 1) * 512],
                    rl_full[rpar][:, :]), inc="sDVE", ser=False)
            if len(st["mul"]) == 16:
                mul_done_prev[0] = st["mul"]
                rl_ev_prev[0] = st["rl"]

        def wo_half(qi, half):
            st = norm_state[qi]
            if half == 0:
                W("act", "sPE", o1_tp_last[0])
            ev = wo_state.get(("evac", qi))
            last, evac_d = dense(wo_bf, ctx_qc, affine=False,
                                 g_ctx_wait=g_ctx_box[0], moving_ready=st["mul"],
                                 out_buf=big_ft, halves=(half,),
                                 evac_d=ev, close=False,
                                 init_last=wo_state.get(("mm", qi), 0))
            wo_state[("evac", qi)] = evac_d
            wo_state[("mm", qi)] = last
            if half == 1:
                dense_prev.append(evac_d[(1, NKC - 1)])
                wo_state[("ev", qi)] = evac_d[(1, NKC - 1)]

        ctx_loads(0)
        norm_chunks(0, range(16))
        wo_half(0, 0)
        wo_half(0, 1)

        for qi in range(NQI):
            wo_ev = wo_state[("ev", qi)]
            if debug and qi == 0:
                p.barrier(p.snap())
                E("sync", lambda e: e.dma_start(out=dbg["dbg_o1"], in_=big_ft[:, :]),
                  inc="dDbg1", inc_n=16)
                p.barrier(p.snap())

            for ti in range(4):
                W("sync", "sDVE", x1_add_done)
                d_x = E("sync", lambda e, qi=qi, ti=ti: e.dma_start(
                    out=x_t[:, :],
                    in_=x_own[512 * qi + 128 * ti: 512 * qi + 128 * ti + 128, :]),
                    inc="dXC", inc_n=16)
                tp = ti % 2
                W("pe", wo_ev[0], wo_ev[1])
                W("pe", "sDVE", o1_war.get(tp, 0))
                tp_block(lambda kc, ti=ti: big_ft[:, kc * 512 + 128 * ti: kc * 512 + 128 * ti + 128],
                         lambda e, g, bank, tp=tp: e.copy(
                             o1_t[tp][:, 512 * g:512 * (g + 1)], bank[:, :]),
                         ps_tc, c_ident_bf, NKC)
                if ti == 3:
                    o1_tp_last[0] = p.cnt["sPE"]
                o1_ready = tp_hist[(NKC - 1) % 2]
                W("dve", "sACT", o1_ready)
                rstd2, negmr2 = ln_stats(o1_t[tp][:, :], sgC[0], stats2, sq_dump2,
                                         war=lnC_war.get(sgC[0] - 4, 0))
                a_ap = E("dve", lambda e, ti=ti, tp=tp, rstd2=rstd2, negmr2=negmr2: e.tensor_scalar(
                    x1b[:, ti * D:(ti + 1) * D], o1_t[tp][:, :], rstd2, negmr2,
                    ALU.mult, ALU.add), inc="sDVE")
                o1_war[tp] = a_ap
                lnC_war[sgC[0]] = a_ap
                sgC[0] += 1
                W("dve", "dXC", d_x)
                x1_add_done = E("dve", lambda e, ti=ti: e.tensor_add(
                    x1b[:, ti * D:(ti + 1) * D], x1b[:, ti * D:(ti + 1) * D],
                    x_t[:, :]), inc="sDVE")
                rstd3, negmr3 = ln_stats(x1b[:, ti * D:(ti + 1) * D], sgC[0], stats2,
                                         sq_dump2, war=x1_add_done)
                W("dve", "sPE", lnx1_tp.get(tp, 0))
                a_l3 = E("dve", lambda e, ti=ti, tp=tp, rstd3=rstd3, negmr3=negmr3: e.tensor_scalar(
                    lnx1_bf[tp][:, :], x1b[:, ti * D:(ti + 1) * D], rstd3, negmr3,
                    ALU.mult, ALU.add), inc="sDVE")
                lnC_war[sgC[0]] = a_l3
                sgC[0] += 1
                W("pe", "sDVE", a_l3)
                tp_block(lambda kc, tp=tp: lnx1_bf[tp][:, 128 * kc:128 * (kc + 1)],
                         lambda e, g, bank, ti=ti: e.copy(
                             lnx1T[:, :].rearrange("p (c t) -> p c t", c=NKC)
                             [:, 4 * g:4 * g + 4, ti * 128:(ti + 1) * 128],
                             bank[:, :].rearrange("p (c q) -> p c q", c=4)),
                         ps_tc, c_ident_bf, NKC)
                lnx1_tp[tp] = p.cnt["sPE"]

            if debug and qi == 0:
                p.barrier(p.snap())
                E("sync", lambda e: e.dma_start(out=dbg["dbg_x1"], in_=x1b[:, :]),
                  inc="dDbg2", inc_n=16)
                p.barrier(p.snap())
            for s_ in ("dSgT", "dSgT1", "dSgT2", "dSgT3"):
                W("sync", s_, p.cnt.get(s_, 0))
            W("act", "sPE", ffn_tp_last)
            ffn_mm_last, ffn_ev = dense(sgnT, lnx1T, affine=True, out_buf=big_fy)
            if debug and qi == 0:
                p.barrier(p.snap())
                E("sync", lambda e: e.dma_start(out=dbg["dbg_y"], in_=big_fy[:, :]),
                  inc="dDbg3", inc_n=16)
                p.barrier(p.snap())

            if qi + 1 < NQI:
                ctx_loads(qi + 1)
            for ti in range(4):
                tp = ti % 2
                W("pe", ffn_ev[0], ffn_ev[1])
                W("pe", "sDVE", y_war.get(tp, 0))
                tp_block(lambda kc, ti=ti: big_fy[:, kc * 512 + 128 * ti: kc * 512 + 128 * ti + 128],
                         lambda e, g, bank, tp=tp: e.copy(
                             y_t[tp][:, 512 * g:512 * (g + 1)], bank[:, :]),
                         ps_tc, c_ident_bf, NKC)
                if ti == 3:
                    ffn_tp_last = p.cnt["sPE"]
                y_ready = tp_hist[(NKC - 1) % 2]
                W("dve", "sACT", y_ready)
                rstd4, negmr4 = ln_stats(y_t[tp][:, :], sgC[0], stats2, sq_dump2,
                                         war=lnC_war.get(sgC[0] - 4, 0))
                W("dve", "dOut", out_store)
                a_l4 = E("dve", lambda e, tp=tp, rstd4=rstd4, negmr4=negmr4: e.tensor_scalar(
                    out_t[:, :], y_t[tp][:, :], rstd4, negmr4,
                    ALU.mult, ALU.add), inc="sDVE")
                y_war[tp] = a_l4
                lnC_war[sgC[0]] = a_l4
                sgC[0] += 1
                d_out = E("dve", lambda e, ti=ti: e.tensor_add(
                    out_t[:, :], out_t[:, :], x1b[:, ti * D:(ti + 1) * D]), inc="sDVE")
                W("sync", "sDVE", d_out)
                out_store = E("sync", lambda e, qi=qi, ti=ti: e.dma_start(
                    out=out_own[512 * qi + 128 * ti: 512 * qi + 128 * ti + 128, :],
                    in_=out_t[:, :]), inc="dOut", inc_n=16)
                if qi + 1 < NQI:
                    norm_chunks(qi + 1, range(4 * ti, 4 * ti + 4))
                    if ti == 2:
                        wo_half(qi + 1, 0)
            if qi + 1 < NQI:
                wo_half(qi + 1, 1)

        for s in list(p.cnt):
            if not s.startswith("d"):
                continue
            q = "gp" if s in ("dIdent", "dWqkv", "dWoCast", "dCC", "dCtx") else "sync"
            W(q, s, p.cnt[s])

        block = st_root.enter_context(nc.Block())

        @block.tensor
        def _(tensor):
            p.replay("pe", tensor, sems)

        @block.scalar
        def _(scalar):
            p.replay("act", scalar, sems)

        @block.vector
        def _(vector):
            p.replay("dve", vector, sems)

        @block.sync
        def _(sync):
            p.replay("sync", sync, sems)

        @block.gpsimd
        def _(gpsimd):
            p.replay("gp", gpsimd, sems)

    return nc


def _host_prep(inputs):
    x = np.asarray(inputs["L0_attn_x"], dtype=np.float32)
    Wq = np.asarray(inputs["L0_attn_Wq"], dtype=np.float32)
    Wk = np.asarray(inputs["L0_attn_Wk"], dtype=np.float32)
    Wv = np.asarray(inputs["L0_attn_Wv"], dtype=np.float32)
    Wo = np.asarray(inputs["L0_attn_Wo"], dtype=np.float32)
    Wf = np.asarray(inputs["L0_ffn_weight"], dtype=np.float32)
    bias = np.asarray(inputs["L0_ffn_bias"], dtype=np.float32)

    j = np.arange(32, dtype=np.float64)
    theta = 10000.0 ** (-2.0 * j / 64.0)
    pos = np.arange(T, dtype=np.float64)
    ang = np.outer(pos, theta)
    cosd, sind = np.cos(ang), np.sin(ang)
    crep = np.empty((128, T), dtype=np.float32)
    srep = np.empty((128, T), dtype=np.float32)
    for p_ in range(128):
        o = ORIG64[p_ % 64]
        if o < 32:
            crep[p_] = cosd[:, o]
            srep[p_] = -sind[:, o]
        else:
            crep[p_] = cosd[:, o - 32]
            srep[p_] = sind[:, o - 32]

    tk = np.arange(128)[:, None]
    tq = np.arange(128)[None, :]
    maskt = np.where(tk <= tq, 0.0, MASKVAL).astype(np.float32)
    identf = np.eye(128, dtype=np.float32)
    ind2 = np.zeros((2, 128), dtype=np.float32)
    ind2[0, 0:64] = 1.0
    ind2[1, 64:128] = 1.0
    consts = np.broadcast_to(
        np.array([[EPS, -0.5, 0.125, 1.0 / D, -1.0, 0.5, 0.0, 0.0]], dtype=np.float32),
        (128, 8)).copy()
    bias_t = np.ascontiguousarray(bias.reshape(16, 128).T)

    in_maps = []
    for c in range(2 * B):
        b, hh = c // 2, c % 2
        qcols = [64 * (16 * hh + lh) + ORIG64[pp] for lh in range(16) for pp in range(64)]
        kcols = []
        for i in range(4):
            base = 64 * (4 * hh + i)
            blk = [base + ORIG64[pp] for pp in range(64)]
            kcols += blk + blk
        vcols = [64 * (4 * hh + i) + d_ for i in range(4) for d_ in range(64)]
        in_maps.append({
            "x_full": np.ascontiguousarray(x[b]),
            "x_own": np.ascontiguousarray(x[b, TH * hh:TH * (hh + 1)]),
            "wq": np.ascontiguousarray(Wq[:, qcols]),
            "wk": np.ascontiguousarray(Wk[:, kcols]),
            "wv": np.ascontiguousarray(Wv[:, vcols]),
            "wo": Wo,
            "wff": Wf,
            "bias_t": bias_t,
            "crep": crep.astype(ml_dtypes.bfloat16),
            "srep": srep.astype(ml_dtypes.bfloat16),
            "maskt": maskt,
            "identf": identf,
            "consts": consts,
            "coreoff": np.array([[hh * 1024 * TH, hh * 16 * TH]], dtype=np.int32),
            "ind2": ind2,
        })
    return in_maps


def kernel(**inputs):
    in_maps = _host_prep(inputs)
    nc = build_nc()
    res = run_bass_kernel_spmd(nc, in_maps, core_ids=list(range(8)))
    out = np.empty((B, T, D), dtype=np.float32)
    for c in range(8):
        b, hh = c // 2, c % 2
        out[b, TH * hh:TH * (hh + 1)] = np.asarray(res.results[c]["out_own"])
    return out


# version 22 (final)
# speedup vs baseline: 1.3679x; 1.3679x over previous
import contextlib
import numpy as np
import ml_dtypes
import concourse.bass as bass
import concourse.mybir as mybir
from concourse.bass_utils import run_bass_kernel_spmd

F32 = mybir.dt.float32
BF16 = mybir.dt.bfloat16
I32 = mybir.dt.int32
AF = mybir.ActivationFunctionType
ALU = mybir.AluOpType
AX = mybir.AxisListType.X

B, T, D = 4, 4096, 2048
TH = T // 2
TB = 512
NTB = T // TB
NQ = 8 * NTB
NKC = D // 128


def set_config(T_new=4096, n_batches=4):
    global B, T, TH, NTB, NQ
    B = n_batches
    T = T_new
    TH = T // 2
    NTB = T // TB
    NQ = 8 * NTB
EPS = 1e-5
MASKVAL = -30000.0
BSAFE = -0.125 * 1.02

ORIG64 = list(range(0, 16)) + list(range(32, 48)) + list(range(16, 32)) + list(range(48, 64))
SHUF_MASK = list(range(16, 32)) + list(range(16))


class Plan:
    ENGS = ("pe", "act", "dve", "sync", "gp")
    SERIAL = {"act": "sACT", "dve": "sDVE", "gp": "sGP"}

    def __init__(self):
        self.ops = {e: [] for e in self.ENGS}
        self.cnt = {}
        self.last_wait = {}

    def wait(self, eng, sem, val):
        if not val:
            return
        key = (eng, sem)
        if self.last_wait.get(key, 0) >= val:
            return
        self.last_wait[key] = val
        self.ops[eng].append(("wait", sem, val))

    def wait_now(self, eng, sem):
        self.wait(eng, sem, self.cnt.get(sem, 0))

    def emit(self, eng, fn, inc=None, inc_n=1, ser=True):
        eng_sem = self.SERIAL.get(eng)
        if eng_sem and (inc is None or inc == eng_sem) and not (
                isinstance(inc, str) and inc.startswith("d")):
            if ser:
                self.wait(eng, eng_sem, self.cnt.get(eng_sem, 0))
            inc, inc_n = eng_sem, 1
        v = None
        if inc is not None:
            v = self.cnt.get(inc, 0) + inc_n
            self.cnt[inc] = v
        self.ops[eng].append(("op", fn, inc, inc_n))
        return v

    def snap(self):
        return dict(self.cnt)

    def barrier(self, snapd):
        for e in self.ENGS:
            for sem, v in snapd.items():
                self.wait(e, sem, v)

    def replay(self, eng, E, sems):
        for item in self.ops[eng]:
            if item[0] == "wait":
                E.wait_ge(sems[item[1]], item[2])
            else:
                ins = item[1](E)
                if item[2] is not None and ins is not None:
                    ins.then_inc(sems[item[2]], item[3])


def build_nc(debug=False, replica_groups=None):
    if replica_groups is None:
        replica_groups = [[2 * i, 2 * i + 1] for i in range(4)]
    nc = bass.Bass("TRN2", target_bir_lowering=False)

    x_full = nc.dram_tensor("x_full", [T, D], F32, kind="ExternalInput").ap()
    x_own = nc.dram_tensor("x_own", [TH, D], F32, kind="ExternalInput").ap()
    wq = nc.dram_tensor("wq", [D, 1024], F32, kind="ExternalInput").ap()
    wk = nc.dram_tensor("wk", [D, 512], F32, kind="ExternalInput").ap()
    wv = nc.dram_tensor("wv", [D, 256], F32, kind="ExternalInput").ap()
    wo = nc.dram_tensor("wo", [D, D], F32, kind="ExternalInput").ap()
    wff = nc.dram_tensor("wff", [D, D], F32, kind="ExternalInput").ap()
    bias_in = nc.dram_tensor("bias_t", [128, 16], F32, kind="ExternalInput").ap()
    crep = nc.dram_tensor("crep", [128, T], BF16, kind="ExternalInput").ap()
    srep = nc.dram_tensor("srep", [128, T], BF16, kind="ExternalInput").ap()
    maskt = nc.dram_tensor("maskt", [128, 128], F32, kind="ExternalInput").ap()
    identf = nc.dram_tensor("identf", [128, 128], F32, kind="ExternalInput").ap()
    consts = nc.dram_tensor("consts", [128, 8], F32, kind="ExternalInput").ap()
    coreoff = nc.dram_tensor("coreoff", [1, 2], I32, kind="ExternalInput").ap()
    ind2_in = nc.dram_tensor("ind2", [2, 128], F32, kind="ExternalInput").ap()

    out_own = nc.dram_tensor("out_own", [TH, D], F32, kind="ExternalOutput").ap()
    wo_bf = nc.dram_tensor("wo_bf", [D, D], BF16).ap()
    sgnT = nc.dram_tensor("sgnT", [D, D], BF16).ap()
    q_dram = nc.dram_tensor("q_dram", [128, 8 * T], BF16).ap()
    CCR = D + 32
    cc_in = nc.dram_tensor("cc_in", [CCR, TH], BF16).ap()
    cc_out = nc.dram_tensor("cc_out", [2 * CCR, TH], BF16).ap()
    dbg = {}
    if debug:
        for nm, sh in [("dbg_k", [128, 4 * T]),
                       ("dbg_v", [128, (T // 128) * 260]),
                       ("dbg_cc", [CCR, TH]), ("dbg_l", [128, 512]),
                       ("dbg_q", [128, 8 * T])]:
            dbg[nm] = nc.dram_tensor(nm, sh, BF16, kind="ExternalOutput").ap()
        dbg["dbg_negB"] = nc.dram_tensor("dbg_negB", [128, NQ], F32, kind="ExternalOutput").ap()
        dbg["dbg_o1"] = nc.dram_tensor("dbg_o1", [128, NKC * 512], BF16, kind="ExternalOutput").ap()
        dbg["dbg_x1"] = nc.dram_tensor("dbg_x1", [128, 4 * D], F32, kind="ExternalOutput").ap()
        dbg["dbg_y"] = nc.dram_tensor("dbg_y", [128, NKC * 512], BF16, kind="ExternalOutput").ap()

    st_root = contextlib.ExitStack()
    with st_root:
        def sb(stk, name, shape, dtype):
            return stk.enter_context(nc.sbuf_tensor(name, shape, dtype))

        def psm(stk, name, shape, dtype=F32):
            return stk.enter_context(nc.psum_tensor(name, shape, dtype))

        sems = {n: st_root.enter_context(nc.semaphore(n))
                for n in ["sPE", "sACT", "sDVE", "sGP", "sCC",
                          "dInit", "dIdent", "dWq", "dWk", "dWv", "dWoCast", "dRope0", "dRope1",
                          "dXA0", "dXA1", "dXA2", "dXA3", "dQS0", "dQS1", "dWff", "dSgT",
                          "dQL0", "dQL1", "dCC", "dCtx", "dWO", "dXC", "dOut", "dWff0", "dWff1", "dSgT2", "dSgT1", "dSgT3",
                          "dDbg0", "dDbg1", "dDbg2", "dDbg3", "dDbg4"]}

        c_ident_bf = sb(st_root, "c_ident_bf", [128, 128], BF16)
        c_ident_f = sb(st_root, "c_ident_f", [128, 128], F32)
        c_mask = sb(st_root, "c_mask", [128, 128], F32)
        c_const = sb(st_root, "c_const", [128, 8], F32)
        c_bias = sb(st_root, "c_bias", [128, 16], F32)
        c_alpha = sb(st_root, "c_alpha", [128, 16], F32)
        c_ones_bf = sb(st_root, "c_ones_bf", [128, 1], BF16)
        c_ones_f = sb(st_root, "c_ones_f", [1, 128], F32)
        c_Bq = sb(st_root, "c_Bq", [1, NQ], F32)
        c_kn2 = sb(st_root, "c_kn2", [1, 8], F32)
        c_negB = sb(st_root, "c_negB", [128, NQ], F32)
        c_off = sb(st_root, "c_off", [1, 2], I32)
        c_ind2 = sb(st_root, "c_ind2", [2, 128], BF16)
        KN = NQ + 8
        TMP = NQ + 16
        c_tiny = sb(st_root, "c_tiny", [1, NQ + 24], F32)

        nc.const_aps.register(F32, EPS, c_const[:, 0:1])
        nc.const_aps.register(F32, -0.5, c_const[:, 1:2])
        nc.const_aps.register(F32, 0.125, c_const[:, 2:3])
        nc.const_aps.register(F32, 1.0 / D, c_const[:, 3:4])
        nc.const_aps.register(F32, -1.0, c_const[:, 4:5])
        nc.const_aps.register(F32, 0.5, c_const[:, 5:6])

        p = Plan()
        W, E, WN = p.wait, p.emit, p.wait_now

        for dst, src in [(c_mask, maskt), (c_const, consts), (c_bias, bias_in),
                         (c_ident_f, identf), (c_off, coreoff)]:
            d_init = E("sync", lambda e, d=dst, s=src: e.dma_start(out=d[:, :], in_=s),
                       inc="dInit", inc_n=16)
        g_ident = E("gp", lambda e: e.dma_start(out=c_ident_bf[:, :], in_=identf),
                    inc="dIdent", inc_n=16)
        for tile, val in [(c_ones_bf, 1.0), (c_ones_f, 1.0), (c_Bq, 0.0), (c_kn2, 0.0)]:
            g_ms = E("gp", lambda e, t=tile, v=val: e.memset(t[:, :], v), inc="sGP")
        g_ind2 = E("gp", lambda e: e.dma_start(out=c_ind2[:, :], in_=ind2_in),
                   inc="dIdent", inc_n=16)
        g_ident = g_ind2
        W("act", "dInit", d_init)
        W("dve", "dInit", d_init)
        W("dve", "sGP", g_ms)
        W("pe", "dInit", d_init)
        W("pe", "sGP", g_ms)

        stAB = st_root.enter_context(contextlib.ExitStack())
        k_dup = sb(stAB, "k_dup", [128, 4 * T], BF16)
        v_tok = sb(stAB, "v_tok", [128, (T // 128) * 260], BF16)
        g_msv = E("gp", lambda e: e.memset(v_tok[:, :], 1.0), inc="sGP")

        stA = st_root.enter_context(contextlib.ExitStack())
        wq_bf = sb(stA, "wq_bf", [128, NKC * 1024], BF16)
        wk_bf = sb(stA, "wk_bf", [128, NKC * 512], BF16)
        wv_bf = sb(stA, "wv_bf", [128, NKC * 256], BF16)
        lnxT = sb(stA, "lnxT", [128, NKC * TB], BF16)
        x_pool = [sb(stA, f"x_pool{i}", [128, D], F32) for i in range(4)]
        lnx_pool = [sb(stA, f"lnx_pool{i}", [128, D], BF16) for i in range(4)]
        stats = sb(stA, "stats", [128, 32], F32)
        rope_c = [sb(stA, f"rope_c{i}", [128, TB], BF16) for i in range(2)]
        rope_s = [sb(stA, f"rope_s{i}", [128, TB], BF16) for i in range(2)]
        qtmp = [sb(stA, f"qtmp{i}", [128, TB], BF16) for i in range(2)]
        qsw = [sb(stA, f"qsw{i}", [128, TB], BF16) for i in range(2)]
        qsq_bf = [sb(stA, f"qsq_bf{i}", [128, TB], BF16) for i in range(2)]
        q_out = [sb(stA, f"q_out{i}", [128, TB], BF16) for i in range(2)]

        ps_tp = [psm(stA, f"ps_tp{i}", [128, 512], BF16) for i in range(2)]
        ps_proj = [psm(stA, f"ps_proj{i}", [128, 512], F32) for i in range(2)]
        ps_v = [psm(stA, f"ps_v{i}", [128, 256], F32) for i in range(2)]
        ps_st = psm(stA, "ps_st", [128, 1024], F32)

        for kc in range(NKC):
            g_wq = E("gp", lambda e, kc=kc: e.dma_start(
                out=wq_bf[:, kc * 1024:(kc + 1) * 1024], in_=wq[kc * 128:(kc + 1) * 128, :]),
                inc="dWq", inc_n=16)
        for kc in range(NKC):
            g_wk = E("gp", lambda e, kc=kc: e.dma_start(
                out=wk_bf[:, kc * 512:(kc + 1) * 512], in_=wk[kc * 128:(kc + 1) * 128, :]),
                inc="dWk", inc_n=16)
        for kc in range(NKC):
            g_wv = E("gp", lambda e, kc=kc: e.dma_start(
                out=wv_bf[:, kc * 256:(kc + 1) * 256], in_=wv[kc * 128:(kc + 1) * 128, :]),
                inc="dWv", inc_n=16)
        for kc in range(NKC):
            E("gp", lambda e, kc=kc: e.dma_start(
                out=wo_bf[kc * 128:(kc + 1) * 128, :], in_=wo[kc * 128:(kc + 1) * 128, :]),
                inc="dWoCast", inc_n=16)

        def stat_c(tile, sg, i):
            o = (sg % 4) * 8 + i
            return tile[:, o:o + 1]

        def ln_stats(tile_src, sg, stattile, dumptile, war=0):
            W("act", "sDVE", war)
            E("dve", lambda e: e.reduce_sum(stat_c(stattile, sg, 0), tile_src, AX))
            E("dve", lambda e: e.tensor_scalar(
                stat_c(stattile, sg, 1), stat_c(stattile, sg, 0), 1.0 / D, None, ALU.mult))
            a_s = E("act", lambda e: e.activation(
                dumptile[:, :], tile_src, AF.Square, accum_out=stat_c(stattile, sg, 2)),
                inc="sACT")
            W("dve", "sACT", a_s)
            E("dve", lambda e: e.tensor_scalar(
                stat_c(stattile, sg, 3), stat_c(stattile, sg, 2), 1.0 / D, None, ALU.mult))
            E("dve", lambda e: e.tensor_scalar(
                stat_c(stattile, sg, 4), stat_c(stattile, sg, 1), stat_c(stattile, sg, 1),
                None, ALU.mult))
            d_v = E("dve", lambda e: e.tensor_sub(
                stat_c(stattile, sg, 5), stat_c(stattile, sg, 3), stat_c(stattile, sg, 4)),
                inc="sDVE")
            W("act", "sDVE", d_v)
            E("act", lambda e: e.activation(
                stat_c(stattile, sg, 6), stat_c(stattile, sg, 5), AF.Ln, bias=EPS))
            a_r = E("act", lambda e: e.activation(
                stat_c(stattile, sg, 6), stat_c(stattile, sg, 6), AF.Exp, scale=-0.5),
                inc="sACT")
            W("dve", "sACT", a_r)
            d_n = E("dve", lambda e: e.tensor_scalar(
                stat_c(stattile, sg, 7), stat_c(stattile, sg, 1), stat_c(stattile, sg, 6),
                -1.0, ALU.mult, ALU.mult), inc="sDVE")
            return stat_c(stattile, sg, 6), stat_c(stattile, sg, 7)

        apply_done = {}
        tp_hist = [0, 0]

        def tp_block(src_fn, dst4_fn, banks, ident, n):
            ng = (n + 3) // 4
            for g in range(ng):
                bank = g % 2
                W("pe", "sACT", tp_hist[bank])
                for sub in range(4):
                    kc = g * 4 + sub
                    if kc >= n:
                        break
                    t_tp = E("pe", lambda e, kc=kc, sub=sub, bank=bank: e.transpose(
                        banks[bank][:, 128 * sub:128 * (sub + 1)], src_fn(kc),
                        ident[:, :]),
                        inc=("sPE" if (sub == 3 or kc == n - 1) else None))
                W("act", "sPE", t_tp)
                tp_hist[bank] = E("act", lambda e, g=g, bank=bank: dst4_fn(e, g, banks[bank]),
                                  inc="sACT", ser=False)
            return tp_hist[(ng - 1) % 2]

        proj_evac = {}
        v_evac = {}
        q_store = {}
        qsq_done = {}
        q_idx = 0
        sgn_prev = None
        sgn_hist = {}
        wprep_tp = [0]
        last_rope_dve = 0
        v_last_mm = 0
        pe_n_hist = {}
        pend_norm = [None]
        d_stt_hist = {}
        norm_idx = [0]
        g_rot = 0
        rope_done = {}

        rope_last = {}
        for tb in range(NTB):
            tpar = tb % 2
            W("sync", "sDVE", rope_last.get(tb - 2, 0))
            E("sync", lambda e, tb=tb, tpar=tpar: e.dma_start(
                out=rope_c[tpar][:, :], in_=crep[:, tb * TB:(tb + 1) * TB]),
                inc=f"dRope{tpar}", inc_n=16)
            d_rope = E("sync", lambda e, tb=tb, tpar=tpar: e.dma_start(
                out=rope_s[tpar][:, :], in_=srep[:, tb * TB:(tb + 1) * TB]),
                inc=f"dRope{tpar}", inc_n=16)

            for st4 in range(4):
                sg = tb * 4 + st4
                slot = sg % 4
                tok0 = sg * 128
                W("sync", "sDVE", apply_done.get(sg - 4, 0))
                d_x = E("sync", lambda e, slot=slot, tok0=tok0: e.dma_start(
                    out=x_pool[slot][:, :], in_=x_full[tok0:tok0 + 128, :]),
                    inc=f"dXA{slot}", inc_n=16)
                W("dve", f"dXA{slot}", d_x)
                W("act", f"dXA{slot}", d_x)
                rstd, negmr = ln_stats(x_pool[slot][:, :], sg, stats, lnx_pool[slot],
                                       war=apply_done.get(sg - 4, 0))
                apply_done[sg] = E("dve", lambda e, slot=slot, rstd=rstd, negmr=negmr: e.tensor_scalar(
                    lnx_pool[slot][:, :], x_pool[slot][:, :], rstd, negmr,
                    ALU.mult, ALU.add), inc="sDVE")
                W("pe", "dIdent", g_ident)
                W("pe", "sDVE", apply_done[sg])
                if st4 == 0:
                    W("act", "sPE", v_last_mm)
                tp_block(lambda kc, slot=slot: lnx_pool[slot][:, 128 * kc:128 * (kc + 1)],
                         lambda e, g, bank, st4=st4: e.copy(
                             lnxT[:, :].rearrange("p (c t) -> p c t", c=NKC)
                             [:, 4 * g:4 * g + 4, st4 * 128:(st4 + 1) * 128],
                             bank[:, :].rearrange("p (c q) -> p c q", c=4)),
                         ps_tp, c_ident_bf, NKC)
            lnxT_ready = tp_hist[(NKC - 1) % 2]

            W("pe", "sACT", lnxT_ready)
            rot = 0

            def emit_norm():
                if pend_norm[0] is None:
                    return
                rp_, asq_, kind_, m_, tb_ = pend_norm[0]
                pend_norm[0] = None
                np_ = norm_idx[0] % 2
                norm_idx[0] += 1
                W("pe", "sACT", asq_)
                W("pe", "sDVE", d_stt_hist.get(np_, 0))
                pe_n = E("pe", lambda e, rp_=rp_, np_=np_: e.matmul(
                    ps_st[0:1, 512 * np_:512 * np_ + TB], c_ones_bf[:, :],
                    qsq_bf[rp_][:, :], start=True, stop=True), inc="sPE")
                pe_n_hist[rp_] = pe_n
                W("dve", "sPE", pe_n)
                if kind_ == "q":
                    d_stt_hist[np_] = E("dve", lambda e, m_=m_, tb_=tb_, np_=np_: e.reduce_max(
                        c_Bq[0:1, m_ * NTB + tb_: m_ * NTB + tb_ + 1],
                        ps_st[0:1, 512 * np_:512 * np_ + TB], AX), inc="sDVE")
                else:
                    E("dve", lambda e, np_=np_: e.reduce_max(
                        c_tiny[0:1, TMP:TMP + 1], ps_st[0:1, 512 * np_:512 * np_ + TB], AX))
                    d_stt_hist[np_] = E("dve", lambda e, m_=m_: e.tensor_max(
                        c_kn2[0:1, m_:m_ + 1], c_kn2[0:1, m_:m_ + 1],
                        c_tiny[0:1, TMP:TMP + 1]), inc="sDVE")

            for kind, nchunk in (("q", 8), ("k", 4)):
                for m in range(nchunk):
                    bank = rot % 2
                    W("pe", "dWq" if kind == "q" else "dWk",
                      g_wq if kind == "q" else g_wk)
                    if proj_evac.get(rot - 2):
                        W("pe", "sACT", proj_evac[rot - 2])
                    for kc in range(NKC):
                        if kind == "q":
                            lhsT = wq_bf[:, kc * 1024 + m * 128: kc * 1024 + (m + 1) * 128]
                        else:
                            lhsT = wk_bf[:, kc * 512 + m * 128: kc * 512 + (m + 1) * 128]
                        pe_mm = E("pe", lambda e, lhsT=lhsT, kc=kc, bank=bank: e.matmul(
                            ps_proj[bank][:, :], lhsT, lnxT[:, kc * TB:(kc + 1) * TB],
                            start=(kc == 0), stop=(kc == NKC - 1)),
                            inc=("sPE" if kc == NKC - 1 else None))
                    emit_norm()
                    rp = g_rot % 2
                    W("act", "sPE", pe_mm)
                    W("act", "sDVE", rope_done.get(g_rot - 2, 0))
                    a_cp = E("act", lambda e, rp=rp, bank=bank: e.copy(
                        qtmp[rp][:, :], ps_proj[bank][:, :]), inc="sACT", ser=False)
                    proj_evac[rot] = a_cp
                    W("dve", "sACT", a_cp)
                    W("dve", f"dRope{tpar}", d_rope)
                    if kind == "q":
                        W("dve", f"dQS{q_idx % 2}", q_store.get(q_idx - 2, 0))
                        W("dve", "sACT", qsq_done.get(rot - 2, 0))
                        dest = q_out[q_idx % 2][:, :]
                    else:
                        dest = k_dup[:, m * T + tb * TB: m * T + (tb + 1) * TB]
                    E("dve", lambda e, rp=rp: e.stream_shuffle(
                        qsw[rp][:, :], qtmp[rp][:, :], SHUF_MASK), ser=False)
                    E("dve", lambda e, rp=rp, tpar=tpar: e.tensor_mul(
                        qtmp[rp][:, :], qtmp[rp][:, :], rope_c[tpar][:, :]), ser=False)
                    E("dve", lambda e, rp=rp, tpar=tpar: e.tensor_mul(
                        qsw[rp][:, :], qsw[rp][:, :], rope_s[tpar][:, :]))
                    d_rp = E("dve", lambda e, rp=rp, dest=dest: e.tensor_add(
                        dest, qtmp[rp][:, :], qsw[rp][:, :]), inc="sDVE")
                    rope_done[g_rot] = d_rp
                    g_rot += 1
                    last_rope_dve = d_rp
                    if kind == "q":
                        W("sync", "sDVE", d_rp)
                        q_store[q_idx] = E("sync", lambda e, m=m, tb=tb, dest=dest: e.dma_start(
                            out=q_dram[:, m * T + tb * TB: m * T + (tb + 1) * TB], in_=dest),
                            inc=f"dQS{q_idx % 2}", inc_n=16)
                        q_idx += 1
                    W("act", "sDVE", d_rp)
                    W("act", "sPE", pe_n_hist.get(rp, 0))
                    a_sq2 = E("act", lambda e, rp=rp, dest=dest: e.activation(
                        qsq_bf[rp][:, :], dest, AF.Square), inc="sACT", ser=False)
                    qsq_done[rot] = a_sq2
                    pend_norm[0] = (rp, a_sq2, kind, m, tb)
                    rot += 1

            emit_norm()
            rope_last[tb] = last_rope_dve
            W("pe", "dWv", g_wv)
            for tt in range(4):
                bank = tt % 2
                if v_evac.get(tt - 2):
                    W("pe", "sACT", v_evac[tt - 2])
                for kc in range(NKC):
                    pe_v = E("pe", lambda e, kc=kc, tt=tt, bank=bank: e.matmul(
                        ps_v[bank][:, :],
                        lnxT[:, kc * TB + tt * 128: kc * TB + (tt + 1) * 128],
                        wv_bf[:, kc * 256:(kc + 1) * 256],
                        start=(kc == 0), stop=(kc == NKC - 1)),
                        inc=("sPE" if kc == NKC - 1 else None))
                W("act", "sPE", pe_v)
                W("act", "sGP", g_msv)
                v_evac[tt] = E("act", lambda e, tb=tb, tt=tt, bank=bank: e.copy(
                    v_tok[:, (tb * 4 + tt) * 260:(tb * 4 + tt + 1) * 260]
                    .rearrange("p (i c) -> p i c", i=4)[:, :, 0:64],
                    ps_v[bank][:, :].rearrange("p (i c) -> p i c", i=4)),
                    inc="sACT", ser=False)
            v_last_mm = pe_v


        WN("act", "sDVE")
        E("act", lambda e: e.activation(c_tiny[0:1, 0:NQ], c_Bq[0:1, :], AF.Ln))
        E("act", lambda e: e.activation(c_tiny[0:1, 0:NQ], c_tiny[0:1, 0:NQ], AF.Exp, scale=0.5))
        E("act", lambda e: e.activation(c_tiny[0:1, KN:KN + 4], c_kn2[0:1, 0:4], AF.Ln))
        a_kn = E("act", lambda e: e.activation(
            c_tiny[0:1, KN:KN + 4], c_tiny[0:1, KN:KN + 4], AF.Exp, scale=0.5), inc="sACT")
        W("dve", "sACT", a_kn)
        for m in range(8):
            E("dve", lambda e, m=m: e.tensor_mul(
                c_tiny[0:1, m * NTB:(m + 1) * NTB], c_tiny[0:1, m * NTB:(m + 1) * NTB],
                c_tiny[0:1, KN + m // 2:KN + 1 + m // 2].broadcast_to([1, NTB])))
        d_nb = E("dve", lambda e: e.tensor_scalar(
            c_tiny[0:1, 0:NQ], c_tiny[0:1, 0:NQ], BSAFE, None, ALU.mult), inc="sDVE")
        W("pe", "sDVE", d_nb)
        pe_nb = E("pe", lambda e: e.matmul(
            ps_st[:, 0:NQ], c_ones_f[0:1, :], c_tiny[0:1, 0:NQ],
            start=True, stop=True), inc="sPE")
        W("act", "sPE", pe_nb)
        a_nb = E("act", lambda e: e.copy(c_negB[:, :], ps_st[:, 0:NQ]), inc="sACT")

        a_end = p.snap()
        if debug:
            p.barrier(a_end)
            for nm, src in (("dbg_k", k_dup), ("dbg_v", v_tok)):
                E("sync", lambda e, nm=nm, src=src: e.dma_start(out=dbg[nm], in_=src[:, :]),
                  inc="dDbg0", inc_n=16)
            E("sync", lambda e: e.dma_start(out=dbg["dbg_negB"], in_=c_negB[:, :]),
              inc="dDbg0", inc_n=16)
            p.barrier(p.snap())

        stA.close()
        stB = st_root.enter_context(contextlib.ExitStack())
        q_slab = [sb(stB, f"q_slab{i}", [128, TB], BF16) for i in range(2)]
        p_buf = [sb(stB, f"p_buf{i}", [128, 1024], BF16) for i in range(4)]
        l_acc = sb(stB, "l_acc", [128, 512], BF16)
        rl_acc = sb(stB, "rl_acc", [128, 512], F32)
        ctx_st = [sb(stB, f"ctx_st{i}", [128, 1024], BF16) for i in range(2)]
        ps_s = [psm(stB, f"ps_s{i}", [128, 1024], F32) for i in range(2)]
        ps_ctx = psm(stB, "ps_ctx", [128, 1024], F32)
        ps_lh1 = psm(stB, "ps_lh1", [128, 1024], F32)

        p.barrier(a_end)
        W("pe", "sACT", a_nb)
        g_msl = E("gp", lambda e: e.memset(l_acc[:, :], 1.0), inc="sGP")
        W("gp", "sGP", g_msl)

        prev_exp_last = 0
        prev_av_last = 0
        prev_tail = 0
        prev_evac = 0
        q_slab_free = {}
        it_b = 0
        for qc in range(NTB):
            half = (qc * 512) // TH
            col0 = (qc * 512) % TH
            for ch in range(8):
                gl = ch // 2
                col = ch * NTB + qc
                jmax = 4 * qc + 4
                qs = it_b % 2
                par_i = it_b % 2
                W("sync", "sPE", q_slab_free.get(it_b - 2, 0))
                d_qs = E("sync", lambda e, qs=qs, ch=ch, qc=qc: e.dma_start(
                    out=q_slab[qs][:, :],
                    in_=q_dram[:, ch * T + 512 * qc: ch * T + 512 * (qc + 1)]),
                    inc=f"dQL{qs}", inc_n=16)
                sc_inc = {}
                sc_src = {}
                ex_inc = {}
                av_inc = {}

                def scores(j, qc=qc, ch=ch, gl=gl, qs=qs, d_qs=d_qs, sc_inc=None,
                           sc_src=None, ex_inc=None):
                    par = j % 2
                    dj = j - 4 * qc
                    qoff = 128 * dj if dj > 0 else 0
                    if ex_inc.get(j - 2):
                        W("pe", "sACT", ex_inc[j - 2])
                    elif j < 2:
                        W("pe", "sACT", prev_exp_last)
                    if j == 0:
                        W("pe", f"dQL{qs}", d_qs)
                    for h in range(2):
                        lo = 64 * h
                        v = E("pe", lambda e, j=j, par=par, qoff=qoff, lo=lo, h=h, qs=qs: e.matmul(
                            ps_s[par][:, 512 * h + qoff: 512 * h + 512],
                            k_dup[lo:lo + 64, gl * T + 128 * j: gl * T + 128 * (j + 1)],
                            q_slab[qs][lo:lo + 64, qoff:512],
                            start=True, stop=True),
                            inc=("sPE" if h == 1 else None))
                    sc_inc[j], sc_src[j] = v, "sPE"
                    if dj >= 0:
                        W("dve", "sPE", v)
                        mview = c_mask[:, :].unsqueeze(1).broadcast_to([128, 2, 128])
                        sview = ps_s[par][:, :].rearrange("p (b n) -> p b n", b=2)[:, :, qoff:qoff + 128]
                        sc_inc[j] = E("dve", lambda e, sview=sview, mview=mview: e.tensor_tensor(
                            sview, sview, mview, ALU.add), inc="sDVE", ser=False)
                        sc_src[j] = "sDVE"

                def expj(j, qc=qc, col=col, sc_inc=None, sc_src=None, ex_inc=None, av_inc=None):
                    par = j % 2
                    dj = j - 4 * qc
                    qoff = 128 * dj if dj > 0 else 0
                    W("act", sc_src[j], sc_inc[j])
                    if av_inc.get(j - 4):
                        W("act", "sPE", av_inc[j - 4])
                    elif j < 4:
                        W("act", "sPE", prev_av_last)
                    sview = ps_s[par][:, :].rearrange("p (b n) -> p b n", b=2)[:, :, qoff:512]
                    pview = p_buf[j % 4][:, :].rearrange("p (b n) -> p b n", b=2)[:, :, qoff:512]
                    ex_inc[j] = E("act", lambda e, sview=sview, pview=pview, col=col: e.activation(
                        pview, sview, AF.Exp, bias=c_negB[:, col:col + 1], scale=0.125),
                        inc="sACT", ser=False)

                def av(j, qc=qc, gl=gl, jmax=jmax, par_i=par_i, prev_evac=prev_evac,
                       ex_inc=None, av_inc=None):
                    par = j % 2
                    dj = j - 4 * qc
                    qoff = 128 * dj if dj > 0 else 0
                    W("pe", "sACT", ex_inc[j])
                    if j == 0:
                        W("pe", "sDVE", prev_evac)
                    E("pe", lambda e, j=j, qoff=qoff, gl=gl: e.matmul(
                        ps_ctx[0:65, qoff:512],
                        v_tok[:, j * 260 + 65 * gl: j * 260 + 65 * gl + 65],
                        p_buf[j % 4][:, qoff:512],
                        start=(j == 0), stop=(j == jmax - 1),
                        tile_position=(0, 0)))
                    E("pe", lambda e, j=j, qoff=qoff, gl=gl: e.matmul(
                        ps_ctx[64:128, 512 + qoff:1024],
                        v_tok[:, j * 260 + 65 * gl: j * 260 + 65 * gl + 64],
                        p_buf[j % 4][:, 512 + qoff:1024],
                        start=(j == 0), stop=(j == jmax - 1),
                        tile_position=(0, 64)))
                    av_inc[j] = E("pe", lambda e, j=j, qoff=qoff, par_i=par_i: e.matmul(
                        ps_lh1[0:1, 512 * par_i + qoff: 512 * par_i + 512],
                        c_ones_bf[:, :],
                        p_buf[j % 4][:, 512 + qoff:1024],
                        start=(j == 0), stop=(j == jmax - 1),
                        tile_position=(0, 0)),
                        inc="sPE")

                kw = dict(sc_inc=sc_inc, sc_src=sc_src, ex_inc=ex_inc)
                scores(0, **kw)
                if jmax > 1:
                    scores(1, **kw)
                for j in range(jmax):
                    expj(j, av_inc=av_inc, **kw)
                    if j >= 1:
                        av(j - 1, ex_inc=ex_inc, av_inc=av_inc)
                    if j + 2 < jmax:
                        scores(j + 2, **kw)
                av(jmax - 1, ex_inc=ex_inc, av_inc=av_inc)

                prev_exp_last = ex_inc[jmax - 1]
                prev_av_last = av_inc[jmax - 1]
                q_slab_free[it_b] = sc_inc[jmax - 1] if sc_src[jmax - 1] == "sPE" else p.cnt["sPE"]
                it_b += 1
                st_t = ctx_st[par_i]
                W("dve", "sPE", av_inc[jmax - 1])
                W("dve", "dCC", prev_tail)
                E("dve", lambda e, st_t=st_t: e.tensor_scalar(
                    st_t[0:65, 0:512], ps_ctx[0:65, 0:512], 1.0, None, ALU.mult),
                    ser=False)
                E("dve", lambda e, st_t=st_t: e.tensor_scalar(
                    st_t[64:128, 512:1024], ps_ctx[64:128, 512:1024],
                    1.0, None, ALU.mult), ser=False)
                prev_evac = E("dve", lambda e, st_t=st_t, par_i=par_i: e.tensor_scalar(
                    st_t[0:1, 512:1024], ps_lh1[0:1, 512 * par_i:512 * par_i + 512],
                    1.0, None, ALU.mult), inc="sDVE", ser=False)
                W("gp", "sDVE", prev_evac)
                row0 = half * 1024 + 128 * ch
                E("gp", lambda e, st_t=st_t, row0=row0, col0=col0: e.dma_start(
                    out=cc_in[row0:row0 + 64, col0:col0 + 512],
                    in_=st_t[0:64, 0:512]), inc="dCC", inc_n=16)
                E("gp", lambda e, st_t=st_t, row0=row0, col0=col0: e.dma_start(
                    out=cc_in[row0 + 64:row0 + 128, col0:col0 + 512],
                    in_=st_t[64:128, 512:1024]), inc="dCC", inc_n=16)
                lrow = qc * 16 + 2 * ch
                E("gp", lambda e, st_t=st_t, lrow=lrow: e.dma_start(
                    out=l_acc[lrow:lrow + 1, :], in_=st_t[64:65, 0:512]),
                    inc="dCC", inc_n=16)
                prev_tail = E("gp", lambda e, st_t=st_t, lrow=lrow: e.dma_start(
                    out=l_acc[lrow + 1:lrow + 2, :], in_=st_t[0:1, 512:1024]),
                    inc="dCC", inc_n=16)

        b_end = p.snap()

        W("dve", "dCC", p.cnt["dCC"])
        d_rl = E("dve", lambda e: e.reciprocal(rl_acc[:, :], l_acc[:, :]), inc="sDVE")
        W("gp", "sDVE", d_rl)
        for qc in range(NTB):
            half = (qc * 512) // TH
            col0 = (qc * 512) % TH
            E("gp", lambda e, qc=qc, half=half, col0=col0: e.dma_start(
                out=cc_in[D + 16 * half: D + 16 * half + 16, col0:col0 + 512],
                in_=rl_acc[16 * qc:16 * qc + 16, :]), inc="dCC", inc_n=16)
        W("gp", "dCC", p.cnt["dCC"])
        if debug:
            W("sync", "dCC", p.cnt["dCC"])
            W("sync", "sDVE", d_rl)
            E("sync", lambda e: e.dma_start(out=dbg["dbg_cc"], in_=cc_in),
              inc="dDbg4", inc_n=16)
            E("sync", lambda e: e.dma_start(out=dbg["dbg_l"], in_=l_acc[:, :]),
              inc="dDbg4", inc_n=16)
            E("sync", lambda e: e.dma_start(out=dbg["dbg_q"], in_=q_dram),
              inc="dDbg4", inc_n=16)
            W("sync", "dDbg4", p.cnt["dDbg4"])
            W("gp", "dDbg4", p.cnt["dDbg4"])
        E("gp", lambda e: e.collective_compute(
            "AllGather", ALU.bypass,
            replica_groups=replica_groups,
            ins=[cc_in[:, :].opt()], outs=[cc_out[:, :].opt()]), inc="sCC")

        W("sync", "dCC", p.cnt["dCC"])
        W("gp", "dCC", p.cnt["dCC"])
        stB.close()
        stAB.close()
        b_end.pop("dCC", None)
        p.barrier(b_end)

        stC = st_root.enter_context(contextlib.ExitStack())
        ctx_qc = sb(stC, "ctx_qc", [128, NKC * 512], BF16)
        wobuf = sb(stC, "wobuf", [128, 8 * D], BF16)
        big_ft = sb(stC, "big_ft", [128, NKC * 512], BF16)
        big_fy = sb(stC, "big_fy", [128, NKC * 512], BF16)
        x1b = sb(stC, "x1b", [128, 4 * D], F32)
        lnx1T = sb(stC, "lnx1T", [128, NKC * 512], BF16)
        x_t = sb(stC, "x_t", [128, D], F32)
        o1_t = [sb(stC, f"o1_t{i}", [128, D], BF16) for i in range(2)]
        y_t = [sb(stC, f"y_t{i}", [128, D], BF16) for i in range(2)]
        out_t = sb(stC, "out_t", [128, D], F32)
        lnx1_bf = [sb(stC, f"lnx1_bf{i}", [128, D], BF16) for i in range(2)]
        stats2 = sb(stC, "stats2", [128, 32], F32)
        sq_dump2 = sb(stC, "sq_dump2", [128, D], BF16)
        rl2h = sb(stC, "rl2h", [2, 8 * 512], BF16)
        wprep0 = sb(stC, "wprep0", [128, D], F32)
        wprep = [wprep0, None]
        wprep_bf = sb(stC, "wprep_bf", [128, D], BF16)
        sgnT_sb = [sb(stC, f"sgnT_sb{i}", [128, D], BF16) for i in range(2)]
        alpha_raw = sb(stC, "alpha_raw", [128, 16], F32)
        rl_full = [sb(stC, f"rl_full{i}", [128, 512], F32) for i in range(2)]
        ps_mm = [psm(stC, f"ps_mm{i}", [128, 512], F32) for i in range(2)]
        ps_tc = [psm(stC, f"ps_tc{i}", [128, 512], BF16) for i in range(2)]
        ps_rl = [psm(stC, f"ps_rl{i}", [128, 512], F32) for i in range(2)]

        W("gp", "dInit", d_init)
        gp_off = [None]
        gp_off2 = [None]
        def _load_off(e):
            v = e.value_load(c_off[0:1, 0:1])
            gp_off[0] = e.snap(v, min_val=0, max_val=1024 * TH)
            v2 = e.value_load(c_off[0:1, 1:2])
            gp_off2[0] = e.snap(v2, min_val=0, max_val=16 * TH)
            return None
        p.ops["gp"].append(("op", _load_off, None, 1))

        wprep[1] = out_t
        sgn_store = {}
        sgn_hist = {}
        wprep_tp = [0]
        for oc in range(16):
            wp = oc % 2
            if sgn_hist.get(wp):
                W("sync", "sDVE", sgn_hist[wp])
            d_w = E("sync", lambda e, oc=oc, wp=wp: e.dma_start(
                out=wprep[wp][:, :], in_=wff[oc * 128:(oc + 1) * 128, :]),
                inc=f"dWff{wp}", inc_n=16)
            W("dve", f"dWff{wp}", d_w)
            W("dve", "sPE", wprep_tp[0])
            E("dve", lambda e, wp=wp: e.tensor_scalar(
                wprep_bf[:, :], wprep[wp][:, :], 0.0, 2.0, ALU.is_ge, ALU.mult))
            d_sg = E("dve", lambda e: e.tensor_scalar(
                wprep_bf[:, :], wprep_bf[:, :], 1.0, None, ALU.subtract), inc="sDVE")
            d_al = E("dve", lambda e, oc=oc, wp=wp: e.reduce_sum(
                alpha_raw[:, oc:oc + 1], wprep[wp][:, :], AX, apply_absolute_value=True),
                inc="sDVE")
            sgn_hist[wp] = d_al
            W("pe", "sDVE", d_sg)
            if sgn_store.get(wp):
                W("pe", f"dSgT{wp}" if wp else "dSgT", sgn_store[wp][0])
                W("pe", f"dSgT{2 + wp}", sgn_store[wp][1])
            tp_block(lambda kc, wp=wp: wprep_bf[:, 128 * kc:128 * (kc + 1)],
                     lambda e, g, bank, wp=wp: e.copy(
                         sgnT_sb[wp][:, 512 * g:512 * (g + 1)], bank[:, :]),
                     ps_tc, c_ident_bf, NKC)
            wprep_tp[0] = p.cnt["sPE"]
            tpl = tp_hist[((NKC + 3) // 4 - 1) % 2]
            W("act", "sACT", tpl)
            W("gp", "sACT", tpl)
            s0 = E("act", lambda e, oc=oc, wp=wp: e.dma_start(
                out=bass.AP(sgnT.tensor, 128 * oc, [[D, 128], [128 * D, 8], [1, 128]]),
                in_=sgnT_sb[wp][:, 0:1024].rearrange("p (c q) -> p c q", c=8)),
                inc=("dSgT1" if wp else "dSgT"), inc_n=16)
            s1 = E("gp", lambda e, oc=oc, wp=wp: e.dma_start(
                out=bass.AP(sgnT.tensor, 128 * oc + 8 * 128 * D,
                            [[D, 128], [128 * D, 8], [1, 128]]),
                in_=sgnT_sb[wp][:, 1024:2048].rearrange("p (c q) -> p c q", c=8)),
                inc=f"dSgT{2 + wp}", inc_n=16)
            sgn_store[wp] = (s0, s1)
        d_alpha = E("dve", lambda e: e.tensor_scalar(
            c_alpha[:, :], alpha_raw[:, :], 1.0 / D, None, ALU.mult), inc="sDVE")

        wo_mm_last = 0
        wobuf_free = 0
        dense_prev = []
        ffn_tp_last = 0
        sgC = [0]
        lnC_war = {}
        o1_war = {}
        y_war = {}
        lnx1_tp = {}
        out_store = 0
        x1_add_done = 0

        def dense(weight_dram, moving, affine, g_ctx_wait=0, moving_ready=None,
                  out_buf=None, halves=(0, 1), evac_d=None, close=True,
                  init_last=0):
            nonlocal wobuf_free
            last = init_last
            if evac_d is None:
                evac_d = {}
            for half in halves:
                W("sync", "sPE", wobuf_free if half == 0 else last)
                slabs = 0
                for k in range(8):
                    slabs = E("sync", lambda e, k=k, half=half, wd=weight_dram: e.dma_start(
                        out=wobuf[:, k * D:(k + 1) * D],
                        in_=wd[(half * 8 + k) * 128:(half * 8 + k + 1) * 128, :]),
                        inc="dWO", inc_n=16)
                for m in range(NKC):
                    bank = m % 2
                    if evac_d.get((half, m - 2)):
                        W("pe", evac_d[(half, m - 2)][0], evac_d[(half, m - 2)][1])
                    elif half == 1 and evac_d.get((0, NKC - 2 + m)):
                        W("pe", evac_d[(0, NKC - 2 + m)][0], evac_d[(0, NKC - 2 + m)][1])
                    elif half == 0 and dense_prev:
                        W("pe", dense_prev[-1][0], dense_prev[-1][1])
                    W("pe", "dWO", slabs)
                    if g_ctx_wait:
                        W("pe", "dCtx", g_ctx_wait)
                    if moving_ready is not None and m == 0:
                        W("pe", "sDVE", moving_ready[half * 8 + 7])
                    for k in range(8):
                        kc2 = half * 8 + k
                        mmv = E("pe", lambda e, k=k, m=m, bank=bank, kc2=kc2: e.matmul(
                            ps_mm[bank][:, :],
                            wobuf[:, k * D + m * 128: k * D + (m + 1) * 128],
                            moving[:, kc2 * 512:(kc2 + 1) * 512],
                            start=(k == 0), stop=(k == 7)),
                            inc=("sPE" if k == 7 else None))
                    last = mmv
                    if half == 0:
                        W("act", "sPE", mmv)
                        ev = E("act", lambda e, m=m, bank=bank, ob=out_buf: e.copy(
                            ob[:, m * 512:(m + 1) * 512], ps_mm[bank][:, :]),
                            inc="sACT", ser=False)
                        evac_d[(half, m)] = ("sACT", ev)
                    else:
                        W("dve", "sPE", mmv)
                        ev = E("dve", lambda e, m=m, bank=bank, ob=out_buf: e.tensor_tensor(
                            ob[:, m * 512:(m + 1) * 512],
                            ob[:, m * 512:(m + 1) * 512],
                            ps_mm[bank][:, :], ALU.add),
                            inc=(None if affine else "sDVE"))
                        if affine:
                            ev = E("dve", lambda e, m=m, ob=out_buf: e.tensor_scalar(
                                ob[:, m * 512:(m + 1) * 512],
                                ob[:, m * 512:(m + 1) * 512],
                                c_alpha[:, m:m + 1], c_bias[:, m:m + 1],
                                ALU.mult, ALU.add), inc="sDVE")
                        evac_d[(half, m)] = ("sDVE", ev)
            wobuf_free = last
            if close:
                dense_prev.append(evac_d[(1, NKC - 1)])
                return last, evac_d[(1, NKC - 1)]
            return last, evac_d

        mul_done_prev = [None]
        rl_ev_prev = [None]
        norm_state = {}
        pe_rl_box = [0]
        g_ctx_box = [0]
        o1_tp_last = [0]
        wo_state = {}
        NQI = TH // 512

        def ctx_loads(qi):
            W("gp", "sCC", 1)
            W("gp", "sPE", wo_state.get(("mm", qi - 1), 0))
            for hi in range(2):
                g_ctx_box[0] = E("gp", lambda e, qi=qi, hi=hi: e.dma_start(
                    out=bass.AP(ctx_qc, hi * 8 * 512,
                                [[NKC * 512, 128], [512, 8], [1, 512]]),
                    in_=bass.AP(cc_out.tensor, gp_off[0] + CCR * TH * hi + 512 * qi,
                                [[TH, 128], [128 * TH, 8], [1, 512]])),
                    inc="dCtx", inc_n=16)

        def norm_chunks(qi, c2s):
            st = norm_state.setdefault(qi, {"mul": {}, "rl": {}})
            for c2 in c2s:
                rpar = c2 % 2
                hi = c2 // 8
                if c2 % 8 == 0:
                    if hi == 1:
                        W("gp", "sPE", pe_rl_box[0])
                    g_ctx_box[0] = E("gp", lambda e, qi=qi, hi=hi: e.dma_start(
                        out=rl2h[0:2, :].rearrange("p (c q) -> p c q", c=8),
                        in_=bass.AP(cc_out.tensor,
                                    gp_off2[0] + CCR * TH * hi + D * TH + 512 * qi,
                                    [[TH, 2], [2 * TH, 8], [1, 512]])),
                        inc="dCtx", inc_n=16)
                W("pe", "dCtx", g_ctx_box[0])
                W("pe", "dIdent", g_ind2)
                if c2 >= 2:
                    W("pe", "sACT", st["rl"][c2 - 2])
                elif rl_ev_prev[0] is not None:
                    W("pe", "sACT", rl_ev_prev[0][14 + c2])
                pe_rl_box[0] = E("pe", lambda e, c2=c2, rpar=rpar: e.matmul(
                    ps_rl[rpar][:, :], c_ind2[:, :],
                    rl2h[0:2, (c2 % 8) * 512:(c2 % 8 + 1) * 512],
                    start=True, stop=True), inc="sPE")
                W("act", "sPE", pe_rl_box[0])
                if c2 >= 2:
                    W("act", "sDVE", st["mul"][c2 - 2])
                elif mul_done_prev[0] is not None:
                    W("act", "sDVE", mul_done_prev[0][14 + c2])
                st["rl"][c2] = E("act", lambda e, rpar=rpar: e.copy(
                    rl_full[rpar][:, :], ps_rl[rpar][:, :]), inc="sACT", ser=False)
                W("dve", "sACT", st["rl"][c2])
                st["mul"][c2] = E("dve", lambda e, c2=c2, rpar=rpar: e.tensor_mul(
                    ctx_qc[:, c2 * 512:(c2 + 1) * 512],
                    ctx_qc[:, c2 * 512:(c2 + 1) * 512],
                    rl_full[rpar][:, :]), inc="sDVE", ser=False)
            if len(st["mul"]) == 16:
                mul_done_prev[0] = st["mul"]
                rl_ev_prev[0] = st["rl"]

        def wo_half(qi, half):
            st = norm_state[qi]
            if half == 0:
                W("act", "sPE", o1_tp_last[0])
            ev = wo_state.get(("evac", qi))
            last, evac_d = dense(wo_bf, ctx_qc, affine=False,
                                 g_ctx_wait=g_ctx_box[0], moving_ready=st["mul"],
                                 out_buf=big_ft, halves=(half,),
                                 evac_d=ev, close=False,
                                 init_last=wo_state.get(("mm", qi), 0))
            wo_state[("evac", qi)] = evac_d
            wo_state[("mm", qi)] = last
            if half == 1:
                dense_prev.append(evac_d[(1, NKC - 1)])
                wo_state[("ev", qi)] = evac_d[(1, NKC - 1)]

        ctx_loads(0)
        norm_chunks(0, range(16))
        wo_half(0, 0)
        wo_half(0, 1)

        for qi in range(NQI):
            wo_ev = wo_state[("ev", qi)]
            if debug and qi == 0:
                p.barrier(p.snap())
                E("sync", lambda e: e.dma_start(out=dbg["dbg_o1"], in_=big_ft[:, :]),
                  inc="dDbg1", inc_n=16)
                p.barrier(p.snap())

            for ti in range(4):
                W("sync", "sDVE", x1_add_done)
                d_x = E("sync", lambda e, qi=qi, ti=ti: e.dma_start(
                    out=x_t[:, :],
                    in_=x_own[512 * qi + 128 * ti: 512 * qi + 128 * ti + 128, :]),
                    inc="dXC", inc_n=16)
                tp = ti % 2
                W("pe", wo_ev[0], wo_ev[1])
                W("pe", "sDVE", o1_war.get(tp, 0))
                tp_block(lambda kc, ti=ti: big_ft[:, kc * 512 + 128 * ti: kc * 512 + 128 * ti + 128],
                         lambda e, g, bank, tp=tp: e.copy(
                             o1_t[tp][:, 512 * g:512 * (g + 1)], bank[:, :]),
                         ps_tc, c_ident_bf, NKC)
                if ti == 3:
                    o1_tp_last[0] = p.cnt["sPE"]
                o1_ready = tp_hist[(NKC - 1) % 2]
                W("dve", "sACT", o1_ready)
                rstd2, negmr2 = ln_stats(o1_t[tp][:, :], sgC[0], stats2, sq_dump2,
                                         war=lnC_war.get(sgC[0] - 4, 0))
                a_ap = E("dve", lambda e, ti=ti, tp=tp, rstd2=rstd2, negmr2=negmr2: e.tensor_scalar(
                    x1b[:, ti * D:(ti + 1) * D], o1_t[tp][:, :], rstd2, negmr2,
                    ALU.mult, ALU.add), inc="sDVE")
                o1_war[tp] = a_ap
                lnC_war[sgC[0]] = a_ap
                sgC[0] += 1
                W("dve", "dXC", d_x)
                x1_add_done = E("dve", lambda e, ti=ti: e.tensor_add(
                    x1b[:, ti * D:(ti + 1) * D], x1b[:, ti * D:(ti + 1) * D],
                    x_t[:, :]), inc="sDVE")
                rstd3, negmr3 = ln_stats(x1b[:, ti * D:(ti + 1) * D], sgC[0], stats2,
                                         sq_dump2, war=x1_add_done)
                W("dve", "sPE", lnx1_tp.get(tp, 0))
                a_l3 = E("dve", lambda e, ti=ti, tp=tp, rstd3=rstd3, negmr3=negmr3: e.tensor_scalar(
                    lnx1_bf[tp][:, :], x1b[:, ti * D:(ti + 1) * D], rstd3, negmr3,
                    ALU.mult, ALU.add), inc="sDVE")
                lnC_war[sgC[0]] = a_l3
                sgC[0] += 1
                W("pe", "sDVE", a_l3)
                tp_block(lambda kc, tp=tp: lnx1_bf[tp][:, 128 * kc:128 * (kc + 1)],
                         lambda e, g, bank, ti=ti: e.copy(
                             lnx1T[:, :].rearrange("p (c t) -> p c t", c=NKC)
                             [:, 4 * g:4 * g + 4, ti * 128:(ti + 1) * 128],
                             bank[:, :].rearrange("p (c q) -> p c q", c=4)),
                         ps_tc, c_ident_bf, NKC)
                lnx1_tp[tp] = p.cnt["sPE"]

            if debug and qi == 0:
                p.barrier(p.snap())
                E("sync", lambda e: e.dma_start(out=dbg["dbg_x1"], in_=x1b[:, :]),
                  inc="dDbg2", inc_n=16)
                p.barrier(p.snap())
            for s_ in ("dSgT", "dSgT1", "dSgT2", "dSgT3"):
                W("sync", s_, p.cnt.get(s_, 0))
            W("act", "sPE", ffn_tp_last)
            ffn_mm_last, ffn_ev = dense(sgnT, lnx1T, affine=True, out_buf=big_fy)
            if debug and qi == 0:
                p.barrier(p.snap())
                E("sync", lambda e: e.dma_start(out=dbg["dbg_y"], in_=big_fy[:, :]),
                  inc="dDbg3", inc_n=16)
                p.barrier(p.snap())

            if qi + 1 < NQI:
                ctx_loads(qi + 1)
            for ti in range(4):
                tp = ti % 2
                W("pe", ffn_ev[0], ffn_ev[1])
                W("pe", "sDVE", y_war.get(tp, 0))
                tp_block(lambda kc, ti=ti: big_fy[:, kc * 512 + 128 * ti: kc * 512 + 128 * ti + 128],
                         lambda e, g, bank, tp=tp: e.copy(
                             y_t[tp][:, 512 * g:512 * (g + 1)], bank[:, :]),
                         ps_tc, c_ident_bf, NKC)
                if ti == 3:
                    ffn_tp_last = p.cnt["sPE"]
                y_ready = tp_hist[(NKC - 1) % 2]
                W("dve", "sACT", y_ready)
                rstd4, negmr4 = ln_stats(y_t[tp][:, :], sgC[0], stats2, sq_dump2,
                                         war=lnC_war.get(sgC[0] - 4, 0))
                W("dve", "dOut", out_store)
                a_l4 = E("dve", lambda e, tp=tp, rstd4=rstd4, negmr4=negmr4: e.tensor_scalar(
                    out_t[:, :], y_t[tp][:, :], rstd4, negmr4,
                    ALU.mult, ALU.add), inc="sDVE")
                y_war[tp] = a_l4
                lnC_war[sgC[0]] = a_l4
                sgC[0] += 1
                d_out = E("dve", lambda e, ti=ti: e.tensor_add(
                    out_t[:, :], out_t[:, :], x1b[:, ti * D:(ti + 1) * D]), inc="sDVE")
                W("sync", "sDVE", d_out)
                out_store = E("sync", lambda e, qi=qi, ti=ti: e.dma_start(
                    out=out_own[512 * qi + 128 * ti: 512 * qi + 128 * ti + 128, :],
                    in_=out_t[:, :]), inc="dOut", inc_n=16)
                if qi + 1 < NQI:
                    norm_chunks(qi + 1, range(4 * ti, 4 * ti + 4))
                    if ti == 2:
                        wo_half(qi + 1, 0)
            if qi + 1 < NQI:
                wo_half(qi + 1, 1)

        for s in list(p.cnt):
            if not s.startswith("d"):
                continue
            q = "gp" if s in ("dIdent", "dWqkv", "dWoCast", "dCC", "dCtx") else "sync"
            W(q, s, p.cnt[s])

        block = st_root.enter_context(nc.Block())

        @block.tensor
        def _(tensor):
            p.replay("pe", tensor, sems)

        @block.scalar
        def _(scalar):
            p.replay("act", scalar, sems)

        @block.vector
        def _(vector):
            p.replay("dve", vector, sems)

        @block.sync
        def _(sync):
            p.replay("sync", sync, sems)

        @block.gpsimd
        def _(gpsimd):
            p.replay("gp", gpsimd, sems)

    return nc


def _host_prep(inputs):
    x = np.asarray(inputs["L0_attn_x"], dtype=np.float32)
    Wq = np.asarray(inputs["L0_attn_Wq"], dtype=np.float32)
    Wk = np.asarray(inputs["L0_attn_Wk"], dtype=np.float32)
    Wv = np.asarray(inputs["L0_attn_Wv"], dtype=np.float32)
    Wo = np.asarray(inputs["L0_attn_Wo"], dtype=np.float32)
    Wf = np.asarray(inputs["L0_ffn_weight"], dtype=np.float32)
    bias = np.asarray(inputs["L0_ffn_bias"], dtype=np.float32)

    j = np.arange(32, dtype=np.float64)
    theta = 10000.0 ** (-2.0 * j / 64.0)
    pos = np.arange(T, dtype=np.float64)
    ang = np.outer(pos, theta)
    cosd, sind = np.cos(ang), np.sin(ang)
    crep = np.empty((128, T), dtype=np.float32)
    srep = np.empty((128, T), dtype=np.float32)
    for p_ in range(128):
        o = ORIG64[p_ % 64]
        if o < 32:
            crep[p_] = cosd[:, o]
            srep[p_] = -sind[:, o]
        else:
            crep[p_] = cosd[:, o - 32]
            srep[p_] = sind[:, o - 32]

    tk = np.arange(128)[:, None]
    tq = np.arange(128)[None, :]
    maskt = np.where(tk <= tq, 0.0, MASKVAL).astype(np.float32)
    identf = np.eye(128, dtype=np.float32)
    ind2 = np.zeros((2, 128), dtype=np.float32)
    ind2[0, 0:64] = 1.0
    ind2[1, 64:128] = 1.0
    consts = np.broadcast_to(
        np.array([[EPS, -0.5, 0.125, 1.0 / D, -1.0, 0.5, 0.0, 0.0]], dtype=np.float32),
        (128, 8)).copy()
    bias_t = np.ascontiguousarray(bias.reshape(16, 128).T)

    in_maps = []
    for c in range(2 * B):
        b, hh = c // 2, c % 2
        qcols = [64 * (16 * hh + lh) + ORIG64[pp] for lh in range(16) for pp in range(64)]
        kcols = []
        for i in range(4):
            base = 64 * (4 * hh + i)
            blk = [base + ORIG64[pp] for pp in range(64)]
            kcols += blk + blk
        vcols = [64 * (4 * hh + i) + d_ for i in range(4) for d_ in range(64)]
        in_maps.append({
            "x_full": np.ascontiguousarray(x[b]),
            "x_own": np.ascontiguousarray(x[b, TH * hh:TH * (hh + 1)]),
            "wq": np.ascontiguousarray(Wq[:, qcols]),
            "wk": np.ascontiguousarray(Wk[:, kcols]),
            "wv": np.ascontiguousarray(Wv[:, vcols]),
            "wo": Wo,
            "wff": Wf,
            "bias_t": bias_t,
            "crep": crep.astype(ml_dtypes.bfloat16),
            "srep": srep.astype(ml_dtypes.bfloat16),
            "maskt": maskt,
            "identf": identf,
            "consts": consts,
            "coreoff": np.array([[hh * 1024 * TH, hh * 16 * TH]], dtype=np.int32),
            "ind2": ind2,
        })
    return in_maps


def kernel(**inputs):
    in_maps = _host_prep(inputs)
    nc = build_nc()
    res = run_bass_kernel_spmd(nc, in_maps, core_ids=list(range(8)))
    out = np.empty((B, T, D), dtype=np.float32)
    for c in range(8):
        b, hh = c // 2, c % 2
        out[b, TH * hh:TH * (hh + 1)] = np.asarray(res.results[c]["out_own"])
    return out


# version 23
# speedup vs baseline: 1.3840x; 1.0118x over previous
import contextlib
import numpy as np
import ml_dtypes
import concourse.bass as bass
import concourse.mybir as mybir
from concourse.bass_utils import run_bass_kernel_spmd

F32 = mybir.dt.float32
BF16 = mybir.dt.bfloat16
I32 = mybir.dt.int32
AF = mybir.ActivationFunctionType
ALU = mybir.AluOpType
AX = mybir.AxisListType.X

B, T, D = 4, 4096, 2048
TH = T // 2
TB = 512
NTB = T // TB
NQ = 8 * NTB
NKC = D // 128


def set_config(T_new=4096, n_batches=4):
    global B, T, TH, NTB, NQ
    B = n_batches
    T = T_new
    TH = T // 2
    NTB = T // TB
    NQ = 8 * NTB
EPS = 1e-5
MASKVAL = -30000.0
BSAFE = -0.125 * 1.02

ORIG64 = list(range(0, 16)) + list(range(32, 48)) + list(range(16, 32)) + list(range(48, 64))
SHUF_MASK = list(range(16, 32)) + list(range(16))


class Plan:
    ENGS = ("pe", "act", "dve", "sync", "gp")
    SERIAL = {"act": "sACT", "dve": "sDVE", "gp": "sGP"}

    def __init__(self):
        self.ops = {e: [] for e in self.ENGS}
        self.cnt = {}
        self.last_wait = {}

    def wait(self, eng, sem, val):
        if not val:
            return
        key = (eng, sem)
        if self.last_wait.get(key, 0) >= val:
            return
        self.last_wait[key] = val
        self.ops[eng].append(("wait", sem, val))

    def wait_now(self, eng, sem):
        self.wait(eng, sem, self.cnt.get(sem, 0))

    def emit(self, eng, fn, inc=None, inc_n=1, ser=True):
        eng_sem = self.SERIAL.get(eng)
        if eng_sem and (inc is None or inc == eng_sem) and not (
                isinstance(inc, str) and inc.startswith("d")):
            if ser:
                self.wait(eng, eng_sem, self.cnt.get(eng_sem, 0))
            inc, inc_n = eng_sem, 1
        v = None
        if inc is not None:
            v = self.cnt.get(inc, 0) + inc_n
            self.cnt[inc] = v
        self.ops[eng].append(("op", fn, inc, inc_n))
        return v

    def snap(self):
        return dict(self.cnt)

    def barrier(self, snapd):
        for e in self.ENGS:
            for sem, v in snapd.items():
                self.wait(e, sem, v)

    def replay(self, eng, E, sems):
        for item in self.ops[eng]:
            if item[0] == "wait":
                E.wait_ge(sems[item[1]], item[2])
            else:
                ins = item[1](E)
                if item[2] is not None and ins is not None:
                    ins.then_inc(sems[item[2]], item[3])


def build_nc(debug=False, replica_groups=None):
    if replica_groups is None:
        replica_groups = [[2 * i, 2 * i + 1] for i in range(4)]
    nc = bass.Bass("TRN2", target_bir_lowering=False)

    x_full = nc.dram_tensor("x_full", [T, D], F32, kind="ExternalInput").ap()
    x_own = nc.dram_tensor("x_own", [TH, D], F32, kind="ExternalInput").ap()
    wq = nc.dram_tensor("wq", [D, 1024], F32, kind="ExternalInput").ap()
    wk = nc.dram_tensor("wk", [D, 512], F32, kind="ExternalInput").ap()
    wv = nc.dram_tensor("wv", [D, 256], F32, kind="ExternalInput").ap()
    wo = nc.dram_tensor("wo", [D, D], F32, kind="ExternalInput").ap()
    wff = nc.dram_tensor("wff", [D, D], F32, kind="ExternalInput").ap()
    bias_in = nc.dram_tensor("bias_t", [128, 16], F32, kind="ExternalInput").ap()
    crep = nc.dram_tensor("crep", [128, T], BF16, kind="ExternalInput").ap()
    srep = nc.dram_tensor("srep", [128, T], BF16, kind="ExternalInput").ap()
    maskt = nc.dram_tensor("maskt", [128, 128], F32, kind="ExternalInput").ap()
    identf = nc.dram_tensor("identf", [128, 128], F32, kind="ExternalInput").ap()
    consts = nc.dram_tensor("consts", [128, 8], F32, kind="ExternalInput").ap()
    coreoff = nc.dram_tensor("coreoff", [1, 2], I32, kind="ExternalInput").ap()
    ind2_in = nc.dram_tensor("ind2", [2, 128], F32, kind="ExternalInput").ap()

    out_own = nc.dram_tensor("out_own", [TH, D], F32, kind="ExternalOutput").ap()
    wo_bf = nc.dram_tensor("wo_bf", [D, D], BF16).ap()
    sgnT = nc.dram_tensor("sgnT", [D, D], BF16).ap()
    q_dram = nc.dram_tensor("q_dram", [128, 8 * T], BF16).ap()
    CCR = D + 32
    cc_in = nc.dram_tensor("cc_in", [CCR, TH], BF16).ap()
    cc_out = nc.dram_tensor("cc_out", [2 * CCR, TH], BF16).ap()
    dbg = {}
    if debug:
        for nm, sh in [("dbg_k", [128, 4 * T]),
                       ("dbg_v", [128, (T // 128) * 260]),
                       ("dbg_cc", [CCR, TH]), ("dbg_l", [128, 512]),
                       ("dbg_q", [128, 8 * T])]:
            dbg[nm] = nc.dram_tensor(nm, sh, BF16, kind="ExternalOutput").ap()
        dbg["dbg_negB"] = nc.dram_tensor("dbg_negB", [128, NQ], F32, kind="ExternalOutput").ap()
        dbg["dbg_o1"] = nc.dram_tensor("dbg_o1", [128, NKC * 512], BF16, kind="ExternalOutput").ap()
        dbg["dbg_x1"] = nc.dram_tensor("dbg_x1", [128, 4 * D], F32, kind="ExternalOutput").ap()
        dbg["dbg_y"] = nc.dram_tensor("dbg_y", [128, NKC * 512], BF16, kind="ExternalOutput").ap()

    st_root = contextlib.ExitStack()
    with st_root:
        def sb(stk, name, shape, dtype):
            return stk.enter_context(nc.sbuf_tensor(name, shape, dtype))

        def psm(stk, name, shape, dtype=F32):
            return stk.enter_context(nc.psum_tensor(name, shape, dtype))

        sems = {n: st_root.enter_context(nc.semaphore(n))
                for n in ["sPE", "sACT", "sDVE", "sGP", "sCC",
                          "dInit", "dIdent", "dWq", "dWk", "dWv", "dWoCast", "dRope0", "dRope1",
                          "dXA0", "dXA1", "dXA2", "dXA3", "dQS0", "dQS1", "dWff", "dSgT",
                          "dQL0", "dQL1", "dCC", "dCtx", "dWO", "dXC", "dOut", "dWff0", "dWff1", "dSgT2", "dSgT1", "dSgT3",
                          "dDbg0", "dDbg1", "dDbg2", "dDbg3", "dDbg4"]}

        c_ident_bf = sb(st_root, "c_ident_bf", [128, 128], BF16)
        c_ident_f = sb(st_root, "c_ident_f", [128, 128], F32)
        c_mask = sb(st_root, "c_mask", [128, 128], F32)
        c_const = sb(st_root, "c_const", [128, 8], F32)
        c_bias = sb(st_root, "c_bias", [128, 16], F32)
        c_alpha = sb(st_root, "c_alpha", [128, 16], F32)
        c_ones_bf = sb(st_root, "c_ones_bf", [128, 1], BF16)
        c_ones_f = sb(st_root, "c_ones_f", [1, 128], F32)
        c_Bq = sb(st_root, "c_Bq", [1, NQ], F32)
        c_kn2 = sb(st_root, "c_kn2", [1, 8], F32)
        c_negB = sb(st_root, "c_negB", [128, NQ], F32)
        c_off = sb(st_root, "c_off", [1, 2], I32)
        c_ind2 = sb(st_root, "c_ind2", [2, 128], BF16)
        KN = NQ + 8
        TMP = NQ + 16
        c_tiny = sb(st_root, "c_tiny", [1, NQ + 24], F32)

        nc.const_aps.register(F32, EPS, c_const[:, 0:1])
        nc.const_aps.register(F32, -0.5, c_const[:, 1:2])
        nc.const_aps.register(F32, 0.125, c_const[:, 2:3])
        nc.const_aps.register(F32, 1.0 / D, c_const[:, 3:4])
        nc.const_aps.register(F32, -1.0, c_const[:, 4:5])
        nc.const_aps.register(F32, 0.5, c_const[:, 5:6])

        p = Plan()
        W, E, WN = p.wait, p.emit, p.wait_now

        for dst, src in [(c_mask, maskt), (c_const, consts), (c_bias, bias_in),
                         (c_ident_f, identf), (c_off, coreoff)]:
            d_init = E("sync", lambda e, d=dst, s=src: e.dma_start(out=d[:, :], in_=s),
                       inc="dInit", inc_n=16)
        g_ident = E("gp", lambda e: e.dma_start(out=c_ident_bf[:, :], in_=identf),
                    inc="dIdent", inc_n=16)
        for tile, val in [(c_ones_bf, 1.0), (c_ones_f, 1.0), (c_Bq, 0.0), (c_kn2, 0.0)]:
            g_ms = E("gp", lambda e, t=tile, v=val: e.memset(t[:, :], v), inc="sGP")
        g_ind2 = E("gp", lambda e: e.dma_start(out=c_ind2[:, :], in_=ind2_in),
                   inc="dIdent", inc_n=16)
        g_ident = g_ind2
        W("act", "dInit", d_init)
        W("dve", "dInit", d_init)
        W("dve", "sGP", g_ms)
        W("pe", "dInit", d_init)
        W("pe", "sGP", g_ms)

        stAB = st_root.enter_context(contextlib.ExitStack())
        k_dup = sb(stAB, "k_dup", [128, 4 * T], BF16)
        v_tok = sb(stAB, "v_tok", [128, (T // 128) * 260], BF16)
        g_msv = E("gp", lambda e: e.memset(v_tok[:, :], 1.0), inc="sGP")

        stA = st_root.enter_context(contextlib.ExitStack())
        wq_bf = sb(stA, "wq_bf", [128, NKC * 1024], BF16)
        wk_bf = sb(stA, "wk_bf", [128, NKC * 512], BF16)
        wv_bf = sb(stA, "wv_bf", [128, NKC * 256], BF16)
        lnxT = sb(stA, "lnxT", [128, NKC * TB], BF16)
        x_pool = [sb(stA, f"x_pool{i}", [128, D], F32) for i in range(4)]
        lnx_pool = [sb(stA, f"lnx_pool{i}", [128, D], BF16) for i in range(4)]
        stats = sb(stA, "stats", [128, 32], F32)
        rope_c = [sb(stA, f"rope_c{i}", [128, TB], BF16) for i in range(2)]
        rope_s = [sb(stA, f"rope_s{i}", [128, TB], BF16) for i in range(2)]
        qtmp = [sb(stA, f"qtmp{i}", [128, TB], BF16) for i in range(2)]
        qsw = [sb(stA, f"qsw{i}", [128, TB], BF16) for i in range(2)]
        qsq_bf = [sb(stA, f"qsq_bf{i}", [128, TB], BF16) for i in range(2)]
        q_out = [sb(stA, f"q_out{i}", [128, TB], BF16) for i in range(2)]

        ps_tp = [psm(stA, f"ps_tp{i}", [128, 512], BF16) for i in range(2)]
        ps_proj = [psm(stA, f"ps_proj{i}", [128, 512], F32) for i in range(2)]
        ps_v = [psm(stA, f"ps_v{i}", [128, 256], F32) for i in range(2)]
        ps_st = psm(stA, "ps_st", [128, 1024], F32)

        for kc in range(NKC):
            g_wq = E("gp", lambda e, kc=kc: e.dma_start(
                out=wq_bf[:, kc * 1024:(kc + 1) * 1024], in_=wq[kc * 128:(kc + 1) * 128, :]),
                inc="dWq", inc_n=16)
        for kc in range(NKC):
            g_wk = E("gp", lambda e, kc=kc: e.dma_start(
                out=wk_bf[:, kc * 512:(kc + 1) * 512], in_=wk[kc * 128:(kc + 1) * 128, :]),
                inc="dWk", inc_n=16)
        for kc in range(NKC):
            g_wv = E("gp", lambda e, kc=kc: e.dma_start(
                out=wv_bf[:, kc * 256:(kc + 1) * 256], in_=wv[kc * 128:(kc + 1) * 128, :]),
                inc="dWv", inc_n=16)
        for kc in range(NKC):
            E("gp", lambda e, kc=kc: e.dma_start(
                out=wo_bf[kc * 128:(kc + 1) * 128, :], in_=wo[kc * 128:(kc + 1) * 128, :]),
                inc="dWoCast", inc_n=16)

        def stat_c(tile, sg, i):
            o = (sg % 4) * 8 + i
            return tile[:, o:o + 1]

        def ln_stats(tile_src, sg, stattile, dumptile, war=0):
            W("act", "sDVE", war)
            E("dve", lambda e: e.reduce_sum(stat_c(stattile, sg, 0), tile_src, AX))
            E("dve", lambda e: e.tensor_scalar(
                stat_c(stattile, sg, 1), stat_c(stattile, sg, 0), 1.0 / D, None, ALU.mult))
            a_s = E("act", lambda e: e.activation(
                dumptile[:, :], tile_src, AF.Square, accum_out=stat_c(stattile, sg, 2)),
                inc="sACT")
            W("dve", "sACT", a_s)
            E("dve", lambda e: e.tensor_scalar(
                stat_c(stattile, sg, 3), stat_c(stattile, sg, 2), 1.0 / D, None, ALU.mult))
            E("dve", lambda e: e.tensor_scalar(
                stat_c(stattile, sg, 4), stat_c(stattile, sg, 1), stat_c(stattile, sg, 1),
                None, ALU.mult))
            d_v = E("dve", lambda e: e.tensor_sub(
                stat_c(stattile, sg, 5), stat_c(stattile, sg, 3), stat_c(stattile, sg, 4)),
                inc="sDVE")
            W("act", "sDVE", d_v)
            E("act", lambda e: e.activation(
                stat_c(stattile, sg, 6), stat_c(stattile, sg, 5), AF.Ln, bias=EPS))
            a_r = E("act", lambda e: e.activation(
                stat_c(stattile, sg, 6), stat_c(stattile, sg, 6), AF.Exp, scale=-0.5),
                inc="sACT")
            W("dve", "sACT", a_r)
            d_n = E("dve", lambda e: e.tensor_scalar(
                stat_c(stattile, sg, 7), stat_c(stattile, sg, 1), stat_c(stattile, sg, 6),
                -1.0, ALU.mult, ALU.mult), inc="sDVE")
            return stat_c(stattile, sg, 6), stat_c(stattile, sg, 7)

        apply_done = {}
        tp_hist = [0, 0]

        def tp_block(src_fn, dst4_fn, banks, ident, n):
            ng = (n + 3) // 4
            for g in range(ng):
                bank = g % 2
                W("pe", "sACT", tp_hist[bank])
                for sub in range(4):
                    kc = g * 4 + sub
                    if kc >= n:
                        break
                    t_tp = E("pe", lambda e, kc=kc, sub=sub, bank=bank: e.transpose(
                        banks[bank][:, 128 * sub:128 * (sub + 1)], src_fn(kc),
                        ident[:, :]),
                        inc=("sPE" if (sub == 3 or kc == n - 1) else None))
                W("act", "sPE", t_tp)
                tp_hist[bank] = E("act", lambda e, g=g, bank=bank: dst4_fn(e, g, banks[bank]),
                                  inc="sACT", ser=False)
            return tp_hist[(ng - 1) % 2]

        proj_evac = {}
        v_evac = {}
        q_store = {}
        qsq_done = {}
        q_idx = 0
        sgn_prev = None
        sgn_hist = {}
        wprep_tp = [0]
        last_rope_dve = 0
        v_last_mm = 0
        pe_n_hist = {}
        pend_norm = [None]
        d_stt_hist = {}
        norm_idx = [0]
        g_rot = 0
        rope_done = {}

        rope_last = {}
        for tb in range(NTB):
            tpar = tb % 2
            W("sync", "sDVE", rope_last.get(tb - 2, 0))
            E("sync", lambda e, tb=tb, tpar=tpar: e.dma_start(
                out=rope_c[tpar][:, :], in_=crep[:, tb * TB:(tb + 1) * TB]),
                inc=f"dRope{tpar}", inc_n=16)
            d_rope = E("sync", lambda e, tb=tb, tpar=tpar: e.dma_start(
                out=rope_s[tpar][:, :], in_=srep[:, tb * TB:(tb + 1) * TB]),
                inc=f"dRope{tpar}", inc_n=16)

            for st4 in range(4):
                sg = tb * 4 + st4
                slot = sg % 4
                tok0 = sg * 128
                W("sync", "sDVE", apply_done.get(sg - 4, 0))
                d_x = E("sync", lambda e, slot=slot, tok0=tok0: e.dma_start(
                    out=x_pool[slot][:, :], in_=x_full[tok0:tok0 + 128, :]),
                    inc=f"dXA{slot}", inc_n=16)
                W("dve", f"dXA{slot}", d_x)
                W("act", f"dXA{slot}", d_x)
                rstd, negmr = ln_stats(x_pool[slot][:, :], sg, stats, lnx_pool[slot],
                                       war=apply_done.get(sg - 4, 0))
                apply_done[sg] = E("dve", lambda e, slot=slot, rstd=rstd, negmr=negmr: e.tensor_scalar(
                    lnx_pool[slot][:, :], x_pool[slot][:, :], rstd, negmr,
                    ALU.mult, ALU.add), inc="sDVE")
                W("pe", "dIdent", g_ident)
                W("pe", "sDVE", apply_done[sg])
                if st4 == 0:
                    W("act", "sPE", v_last_mm)
                tp_block(lambda kc, slot=slot: lnx_pool[slot][:, 128 * kc:128 * (kc + 1)],
                         lambda e, g, bank, st4=st4: e.copy(
                             lnxT[:, :].rearrange("p (c t) -> p c t", c=NKC)
                             [:, 4 * g:4 * g + 4, st4 * 128:(st4 + 1) * 128],
                             bank[:, :].rearrange("p (c q) -> p c q", c=4)),
                         ps_tp, c_ident_bf, NKC)
            lnxT_ready = tp_hist[(NKC - 1) % 2]

            W("pe", "sACT", lnxT_ready)
            rot = 0

            def emit_norm():
                if pend_norm[0] is None:
                    return
                rp_, asq_, kind_, m_, tb_ = pend_norm[0]
                pend_norm[0] = None
                np_ = norm_idx[0] % 2
                norm_idx[0] += 1
                W("pe", "sACT", asq_)
                W("pe", "sDVE", d_stt_hist.get(np_, 0))
                pe_n = E("pe", lambda e, rp_=rp_, np_=np_: e.matmul(
                    ps_st[0:1, 512 * np_:512 * np_ + TB], c_ones_bf[:, :],
                    qsq_bf[rp_][:, :], start=True, stop=True), inc="sPE")
                pe_n_hist[rp_] = pe_n
                W("dve", "sPE", pe_n)
                if kind_ == "q":
                    d_stt_hist[np_] = E("dve", lambda e, m_=m_, tb_=tb_, np_=np_: e.reduce_max(
                        c_Bq[0:1, m_ * NTB + tb_: m_ * NTB + tb_ + 1],
                        ps_st[0:1, 512 * np_:512 * np_ + TB], AX), inc="sDVE")
                else:
                    E("dve", lambda e, np_=np_: e.reduce_max(
                        c_tiny[0:1, TMP:TMP + 1], ps_st[0:1, 512 * np_:512 * np_ + TB], AX))
                    d_stt_hist[np_] = E("dve", lambda e, m_=m_: e.tensor_max(
                        c_kn2[0:1, m_:m_ + 1], c_kn2[0:1, m_:m_ + 1],
                        c_tiny[0:1, TMP:TMP + 1]), inc="sDVE")

            for kind, nchunk in (("q", 8), ("k", 4)):
                for m in range(nchunk):
                    bank = rot % 2
                    W("pe", "dWq" if kind == "q" else "dWk",
                      g_wq if kind == "q" else g_wk)
                    if proj_evac.get(rot - 2):
                        W("pe", "sACT", proj_evac[rot - 2])
                    for kc in range(NKC):
                        if kind == "q":
                            lhsT = wq_bf[:, kc * 1024 + m * 128: kc * 1024 + (m + 1) * 128]
                        else:
                            lhsT = wk_bf[:, kc * 512 + m * 128: kc * 512 + (m + 1) * 128]
                        pe_mm = E("pe", lambda e, lhsT=lhsT, kc=kc, bank=bank: e.matmul(
                            ps_proj[bank][:, :], lhsT, lnxT[:, kc * TB:(kc + 1) * TB],
                            start=(kc == 0), stop=(kc == NKC - 1)),
                            inc=("sPE" if kc == NKC - 1 else None))
                    emit_norm()
                    rp = g_rot % 2
                    W("act", "sPE", pe_mm)
                    W("act", "sDVE", rope_done.get(g_rot - 2, 0))
                    a_cp = E("act", lambda e, rp=rp, bank=bank: e.copy(
                        qtmp[rp][:, :], ps_proj[bank][:, :]), inc="sACT", ser=False)
                    proj_evac[rot] = a_cp
                    W("dve", "sACT", a_cp)
                    W("dve", f"dRope{tpar}", d_rope)
                    if kind == "q":
                        W("dve", f"dQS{q_idx % 2}", q_store.get(q_idx - 2, 0))
                        W("dve", "sACT", qsq_done.get(rot - 2, 0))
                        dest = q_out[q_idx % 2][:, :]
                    else:
                        dest = k_dup[:, m * T + tb * TB: m * T + (tb + 1) * TB]
                    E("dve", lambda e, rp=rp: e.stream_shuffle(
                        qsw[rp][:, :], qtmp[rp][:, :], SHUF_MASK), ser=False)
                    E("dve", lambda e, rp=rp, tpar=tpar: e.tensor_mul(
                        qtmp[rp][:, :], qtmp[rp][:, :], rope_c[tpar][:, :]), ser=False)
                    E("dve", lambda e, rp=rp, tpar=tpar: e.tensor_mul(
                        qsw[rp][:, :], qsw[rp][:, :], rope_s[tpar][:, :]))
                    d_rp = E("dve", lambda e, rp=rp, dest=dest: e.tensor_add(
                        dest, qtmp[rp][:, :], qsw[rp][:, :]), inc="sDVE")
                    rope_done[g_rot] = d_rp
                    g_rot += 1
                    last_rope_dve = d_rp
                    if kind == "q":
                        W("sync", "sDVE", d_rp)
                        q_store[q_idx] = E("sync", lambda e, m=m, tb=tb, dest=dest: e.dma_start(
                            out=q_dram[:, m * T + tb * TB: m * T + (tb + 1) * TB], in_=dest),
                            inc=f"dQS{q_idx % 2}", inc_n=16)
                        q_idx += 1
                    W("act", "sDVE", d_rp)
                    W("act", "sPE", pe_n_hist.get(rp, 0))
                    a_sq2 = E("act", lambda e, rp=rp, dest=dest: e.activation(
                        qsq_bf[rp][:, :], dest, AF.Square), inc="sACT", ser=False)
                    qsq_done[rot] = a_sq2
                    pend_norm[0] = (rp, a_sq2, kind, m, tb)
                    rot += 1

            emit_norm()
            rope_last[tb] = last_rope_dve
            W("pe", "dWv", g_wv)
            for tt in range(4):
                bank = tt % 2
                if v_evac.get(tt - 2):
                    W("pe", "sACT", v_evac[tt - 2])
                for kc in range(NKC):
                    pe_v = E("pe", lambda e, kc=kc, tt=tt, bank=bank: e.matmul(
                        ps_v[bank][:, :],
                        lnxT[:, kc * TB + tt * 128: kc * TB + (tt + 1) * 128],
                        wv_bf[:, kc * 256:(kc + 1) * 256],
                        start=(kc == 0), stop=(kc == NKC - 1)),
                        inc=("sPE" if kc == NKC - 1 else None))
                W("act", "sPE", pe_v)
                W("act", "sGP", g_msv)
                v_evac[tt] = E("act", lambda e, tb=tb, tt=tt, bank=bank: e.copy(
                    v_tok[:, (tb * 4 + tt) * 260:(tb * 4 + tt + 1) * 260]
                    .rearrange("p (i c) -> p i c", i=4)[:, :, 0:64],
                    ps_v[bank][:, :].rearrange("p (i c) -> p i c", i=4)),
                    inc="sACT", ser=False)
            v_last_mm = pe_v


        WN("act", "sDVE")
        E("act", lambda e: e.activation(c_tiny[0:1, 0:NQ], c_Bq[0:1, :], AF.Ln))
        E("act", lambda e: e.activation(c_tiny[0:1, 0:NQ], c_tiny[0:1, 0:NQ], AF.Exp, scale=0.5))
        E("act", lambda e: e.activation(c_tiny[0:1, KN:KN + 4], c_kn2[0:1, 0:4], AF.Ln))
        a_kn = E("act", lambda e: e.activation(
            c_tiny[0:1, KN:KN + 4], c_tiny[0:1, KN:KN + 4], AF.Exp, scale=0.5), inc="sACT")
        W("dve", "sACT", a_kn)
        for m in range(8):
            E("dve", lambda e, m=m: e.tensor_mul(
                c_tiny[0:1, m * NTB:(m + 1) * NTB], c_tiny[0:1, m * NTB:(m + 1) * NTB],
                c_tiny[0:1, KN + m // 2:KN + 1 + m // 2].broadcast_to([1, NTB])))
        d_nb = E("dve", lambda e: e.tensor_scalar(
            c_tiny[0:1, 0:NQ], c_tiny[0:1, 0:NQ], BSAFE, None, ALU.mult), inc="sDVE")
        W("pe", "sDVE", d_nb)
        pe_nb = E("pe", lambda e: e.matmul(
            ps_st[:, 0:NQ], c_ones_f[0:1, :], c_tiny[0:1, 0:NQ],
            start=True, stop=True), inc="sPE")
        W("act", "sPE", pe_nb)
        a_nb = E("act", lambda e: e.copy(c_negB[:, :], ps_st[:, 0:NQ]), inc="sACT")

        a_end = p.snap()
        if debug:
            p.barrier(a_end)
            for nm, src in (("dbg_k", k_dup), ("dbg_v", v_tok)):
                E("sync", lambda e, nm=nm, src=src: e.dma_start(out=dbg[nm], in_=src[:, :]),
                  inc="dDbg0", inc_n=16)
            E("sync", lambda e: e.dma_start(out=dbg["dbg_negB"], in_=c_negB[:, :]),
              inc="dDbg0", inc_n=16)
            p.barrier(p.snap())

        stA.close()
        stB = st_root.enter_context(contextlib.ExitStack())
        q_slab = [sb(stB, f"q_slab{i}", [128, TB], BF16) for i in range(2)]
        p_buf = [sb(stB, f"p_buf{i}", [128, 1024], BF16) for i in range(4)]
        l_acc = sb(stB, "l_acc", [128, 512], BF16)
        rl_acc = sb(stB, "rl_acc", [128, 512], F32)
        ctx_st = [sb(stB, f"ctx_st{i}", [128, 1024], BF16) for i in range(2)]
        ps_s = [psm(stB, f"ps_s{i}", [128, 1024], F32) for i in range(2)]
        ps_ctx = psm(stB, "ps_ctx", [128, 1024], F32)
        ps_lh1 = psm(stB, "ps_lh1", [128, 1024], F32)

        p.barrier(a_end)
        W("pe", "sACT", a_nb)
        g_msl = E("gp", lambda e: e.memset(l_acc[:, :], 1.0), inc="sGP")
        W("gp", "sGP", g_msl)

        prev_exp_last = 0
        prev_av_last = 0
        prev_tail = 0
        prev_evac = 0
        q_slab_free = {}
        it_b = 0
        for qc in range(NTB):
            half = (qc * 512) // TH
            col0 = (qc * 512) % TH
            for ch in range(8):
                gl = ch // 2
                col = ch * NTB + qc
                jmax = 4 * qc + 4
                qs = it_b % 2
                par_i = it_b % 2
                W("sync", "sPE", q_slab_free.get(it_b - 2, 0))
                d_qs = E("sync", lambda e, qs=qs, ch=ch, qc=qc: e.dma_start(
                    out=q_slab[qs][:, :],
                    in_=q_dram[:, ch * T + 512 * qc: ch * T + 512 * (qc + 1)]),
                    inc=f"dQL{qs}", inc_n=16)
                sc_inc = {}
                sc_src = {}
                ex_inc = {}
                av_inc = {}

                def scores(j, qc=qc, ch=ch, gl=gl, qs=qs, d_qs=d_qs, sc_inc=None,
                           sc_src=None, ex_inc=None):
                    par = j % 2
                    dj = j - 4 * qc
                    qoff = 128 * dj if dj > 0 else 0
                    if ex_inc.get(j - 2):
                        W("pe", "sACT", ex_inc[j - 2])
                    elif j < 2:
                        W("pe", "sACT", prev_exp_last)
                    if j == 0:
                        W("pe", f"dQL{qs}", d_qs)
                    for h in range(2):
                        lo = 64 * h
                        v = E("pe", lambda e, j=j, par=par, qoff=qoff, lo=lo, h=h, qs=qs: e.matmul(
                            ps_s[par][:, 512 * h + qoff: 512 * h + 512],
                            k_dup[lo:lo + 64, gl * T + 128 * j: gl * T + 128 * (j + 1)],
                            q_slab[qs][lo:lo + 64, qoff:512],
                            start=True, stop=True),
                            inc=("sPE" if h == 1 else None))
                    sc_inc[j], sc_src[j] = v, "sPE"
                    if dj >= 0:
                        W("dve", "sPE", v)
                        mview = c_mask[:, :].unsqueeze(1).broadcast_to([128, 2, 128])
                        sview = ps_s[par][:, :].rearrange("p (b n) -> p b n", b=2)[:, :, qoff:qoff + 128]
                        sc_inc[j] = E("dve", lambda e, sview=sview, mview=mview: e.tensor_tensor(
                            sview, sview, mview, ALU.add), inc="sDVE", ser=False)
                        sc_src[j] = "sDVE"

                def expj(j, qc=qc, col=col, sc_inc=None, sc_src=None, ex_inc=None, av_inc=None):
                    par = j % 2
                    dj = j - 4 * qc
                    qoff = 128 * dj if dj > 0 else 0
                    W("act", sc_src[j], sc_inc[j])
                    if av_inc.get(j - 4):
                        W("act", "sPE", av_inc[j - 4])
                    elif j < 4:
                        W("act", "sPE", prev_av_last)
                    sview = ps_s[par][:, :].rearrange("p (b n) -> p b n", b=2)[:, :, qoff:512]
                    pview = p_buf[j % 4][:, :].rearrange("p (b n) -> p b n", b=2)[:, :, qoff:512]
                    ex_inc[j] = E("act", lambda e, sview=sview, pview=pview, col=col: e.activation(
                        pview, sview, AF.Exp, bias=c_negB[:, col:col + 1], scale=0.125),
                        inc="sACT", ser=False)

                def av(j, qc=qc, gl=gl, jmax=jmax, par_i=par_i, prev_evac=prev_evac,
                       ex_inc=None, av_inc=None):
                    par = j % 2
                    dj = j - 4 * qc
                    qoff = 128 * dj if dj > 0 else 0
                    W("pe", "sACT", ex_inc[j])
                    if j == 0:
                        W("pe", "sDVE", prev_evac)
                    E("pe", lambda e, j=j, qoff=qoff, gl=gl: e.matmul(
                        ps_ctx[0:65, qoff:512],
                        v_tok[:, j * 260 + 65 * gl: j * 260 + 65 * gl + 65],
                        p_buf[j % 4][:, qoff:512],
                        start=(j == 0), stop=(j == jmax - 1),
                        tile_position=(0, 0)))
                    E("pe", lambda e, j=j, qoff=qoff, gl=gl: e.matmul(
                        ps_ctx[64:128, 512 + qoff:1024],
                        v_tok[:, j * 260 + 65 * gl: j * 260 + 65 * gl + 64],
                        p_buf[j % 4][:, 512 + qoff:1024],
                        start=(j == 0), stop=(j == jmax - 1),
                        tile_position=(0, 64)))
                    av_inc[j] = E("pe", lambda e, j=j, qoff=qoff, par_i=par_i: e.matmul(
                        ps_lh1[0:1, 512 * par_i + qoff: 512 * par_i + 512],
                        c_ones_bf[:, :],
                        p_buf[j % 4][:, 512 + qoff:1024],
                        start=(j == 0), stop=(j == jmax - 1),
                        tile_position=(0, 0)),
                        inc="sPE")

                kw = dict(sc_inc=sc_inc, sc_src=sc_src, ex_inc=ex_inc)
                scores(0, **kw)
                if jmax > 1:
                    scores(1, **kw)
                for j in range(jmax):
                    expj(j, av_inc=av_inc, **kw)
                    if j >= 2:
                        av(j - 2, ex_inc=ex_inc, av_inc=av_inc)
                    if j + 2 < jmax:
                        scores(j + 2, **kw)
                av(jmax - 2, ex_inc=ex_inc, av_inc=av_inc)
                av(jmax - 1, ex_inc=ex_inc, av_inc=av_inc)

                prev_exp_last = ex_inc[jmax - 1]
                prev_av_last = av_inc[jmax - 1]
                q_slab_free[it_b] = sc_inc[jmax - 1] if sc_src[jmax - 1] == "sPE" else p.cnt["sPE"]
                it_b += 1
                st_t = ctx_st[par_i]
                W("dve", "sPE", av_inc[jmax - 1])
                W("dve", "dCC", prev_tail)
                E("dve", lambda e, st_t=st_t: e.tensor_scalar(
                    st_t[0:65, 0:512], ps_ctx[0:65, 0:512], 1.0, None, ALU.mult),
                    ser=False)
                E("dve", lambda e, st_t=st_t: e.tensor_scalar(
                    st_t[64:128, 512:1024], ps_ctx[64:128, 512:1024],
                    1.0, None, ALU.mult), ser=False)
                prev_evac = E("dve", lambda e, st_t=st_t, par_i=par_i: e.tensor_scalar(
                    st_t[0:1, 512:1024], ps_lh1[0:1, 512 * par_i:512 * par_i + 512],
                    1.0, None, ALU.mult), inc="sDVE", ser=False)
                W("gp", "sDVE", prev_evac)
                row0 = half * 1024 + 128 * ch
                E("gp", lambda e, st_t=st_t, row0=row0, col0=col0: e.dma_start(
                    out=cc_in[row0:row0 + 64, col0:col0 + 512],
                    in_=st_t[0:64, 0:512]), inc="dCC", inc_n=16)
                E("gp", lambda e, st_t=st_t, row0=row0, col0=col0: e.dma_start(
                    out=cc_in[row0 + 64:row0 + 128, col0:col0 + 512],
                    in_=st_t[64:128, 512:1024]), inc="dCC", inc_n=16)
                lrow = qc * 16 + 2 * ch
                E("gp", lambda e, st_t=st_t, lrow=lrow: e.dma_start(
                    out=l_acc[lrow:lrow + 1, :], in_=st_t[64:65, 0:512]),
                    inc="dCC", inc_n=16)
                prev_tail = E("gp", lambda e, st_t=st_t, lrow=lrow: e.dma_start(
                    out=l_acc[lrow + 1:lrow + 2, :], in_=st_t[0:1, 512:1024]),
                    inc="dCC", inc_n=16)

        b_end = p.snap()

        W("dve", "dCC", p.cnt["dCC"])
        d_rl = E("dve", lambda e: e.reciprocal(rl_acc[:, :], l_acc[:, :]), inc="sDVE")
        W("gp", "sDVE", d_rl)
        for qc in range(NTB):
            half = (qc * 512) // TH
            col0 = (qc * 512) % TH
            E("gp", lambda e, qc=qc, half=half, col0=col0: e.dma_start(
                out=cc_in[D + 16 * half: D + 16 * half + 16, col0:col0 + 512],
                in_=rl_acc[16 * qc:16 * qc + 16, :]), inc="dCC", inc_n=16)
        W("gp", "dCC", p.cnt["dCC"])
        if debug:
            W("sync", "dCC", p.cnt["dCC"])
            W("sync", "sDVE", d_rl)
            E("sync", lambda e: e.dma_start(out=dbg["dbg_cc"], in_=cc_in),
              inc="dDbg4", inc_n=16)
            E("sync", lambda e: e.dma_start(out=dbg["dbg_l"], in_=l_acc[:, :]),
              inc="dDbg4", inc_n=16)
            E("sync", lambda e: e.dma_start(out=dbg["dbg_q"], in_=q_dram),
              inc="dDbg4", inc_n=16)
            W("sync", "dDbg4", p.cnt["dDbg4"])
            W("gp", "dDbg4", p.cnt["dDbg4"])
        E("gp", lambda e: e.collective_compute(
            "AllGather", ALU.bypass,
            replica_groups=replica_groups,
            ins=[cc_in[:, :].opt()], outs=[cc_out[:, :].opt()]), inc="sCC")

        W("sync", "dCC", p.cnt["dCC"])
        W("gp", "dCC", p.cnt["dCC"])
        stB.close()
        stAB.close()
        b_end.pop("dCC", None)
        p.barrier(b_end)

        stC = st_root.enter_context(contextlib.ExitStack())
        ctx_qc = sb(stC, "ctx_qc", [128, NKC * 512], BF16)
        wobuf = sb(stC, "wobuf", [128, 8 * D], BF16)
        big_ft = sb(stC, "big_ft", [128, NKC * 512], BF16)
        big_fy = sb(stC, "big_fy", [128, NKC * 512], BF16)
        x1b = sb(stC, "x1b", [128, 4 * D], F32)
        lnx1T = sb(stC, "lnx1T", [128, NKC * 512], BF16)
        x_t = sb(stC, "x_t", [128, D], F32)
        o1_t = [sb(stC, f"o1_t{i}", [128, D], BF16) for i in range(2)]
        y_t = [sb(stC, f"y_t{i}", [128, D], BF16) for i in range(2)]
        out_t = sb(stC, "out_t", [128, D], F32)
        lnx1_bf = [sb(stC, f"lnx1_bf{i}", [128, D], BF16) for i in range(2)]
        stats2 = sb(stC, "stats2", [128, 32], F32)
        sq_dump2 = sb(stC, "sq_dump2", [128, D], BF16)
        rl2h = sb(stC, "rl2h", [2, 8 * 512], BF16)
        wprep0 = sb(stC, "wprep0", [128, D], F32)
        wprep = [wprep0, None]
        wprep_bf = sb(stC, "wprep_bf", [128, D], BF16)
        sgnT_sb = [sb(stC, f"sgnT_sb{i}", [128, D], BF16) for i in range(2)]
        alpha_raw = sb(stC, "alpha_raw", [128, 16], F32)
        rl_full = [sb(stC, f"rl_full{i}", [128, 512], F32) for i in range(2)]
        ps_mm = [psm(stC, f"ps_mm{i}", [128, 512], F32) for i in range(2)]
        ps_tc = [psm(stC, f"ps_tc{i}", [128, 512], BF16) for i in range(2)]
        ps_rl = [psm(stC, f"ps_rl{i}", [128, 512], F32) for i in range(2)]

        W("gp", "dInit", d_init)
        gp_off = [None]
        gp_off2 = [None]
        def _load_off(e):
            v = e.value_load(c_off[0:1, 0:1])
            gp_off[0] = e.snap(v, min_val=0, max_val=1024 * TH)
            v2 = e.value_load(c_off[0:1, 1:2])
            gp_off2[0] = e.snap(v2, min_val=0, max_val=16 * TH)
            return None
        p.ops["gp"].append(("op", _load_off, None, 1))

        wprep[1] = out_t
        sgn_store = {}
        sgn_hist = {}
        wprep_tp = [0]
        for oc in range(16):
            wp = oc % 2
            if sgn_hist.get(wp):
                W("sync", "sDVE", sgn_hist[wp])
            d_w = E("sync", lambda e, oc=oc, wp=wp: e.dma_start(
                out=wprep[wp][:, :], in_=wff[oc * 128:(oc + 1) * 128, :]),
                inc=f"dWff{wp}", inc_n=16)
            W("dve", f"dWff{wp}", d_w)
            W("dve", "sPE", wprep_tp[0])
            E("dve", lambda e, wp=wp: e.tensor_scalar(
                wprep_bf[:, :], wprep[wp][:, :], 0.0, 2.0, ALU.is_ge, ALU.mult))
            d_sg = E("dve", lambda e: e.tensor_scalar(
                wprep_bf[:, :], wprep_bf[:, :], 1.0, None, ALU.subtract), inc="sDVE")
            d_al = E("dve", lambda e, oc=oc, wp=wp: e.reduce_sum(
                alpha_raw[:, oc:oc + 1], wprep[wp][:, :], AX, apply_absolute_value=True),
                inc="sDVE")
            sgn_hist[wp] = d_al
            W("pe", "sDVE", d_sg)
            if sgn_store.get(wp):
                W("pe", f"dSgT{wp}" if wp else "dSgT", sgn_store[wp][0])
                W("pe", f"dSgT{2 + wp}", sgn_store[wp][1])
            tp_block(lambda kc, wp=wp: wprep_bf[:, 128 * kc:128 * (kc + 1)],
                     lambda e, g, bank, wp=wp: e.copy(
                         sgnT_sb[wp][:, 512 * g:512 * (g + 1)], bank[:, :]),
                     ps_tc, c_ident_bf, NKC)
            wprep_tp[0] = p.cnt["sPE"]
            tpl = tp_hist[((NKC + 3) // 4 - 1) % 2]
            W("act", "sACT", tpl)
            W("gp", "sACT", tpl)
            s0 = E("act", lambda e, oc=oc, wp=wp: e.dma_start(
                out=bass.AP(sgnT.tensor, 128 * oc, [[D, 128], [128 * D, 8], [1, 128]]),
                in_=sgnT_sb[wp][:, 0:1024].rearrange("p (c q) -> p c q", c=8)),
                inc=("dSgT1" if wp else "dSgT"), inc_n=16)
            s1 = E("gp", lambda e, oc=oc, wp=wp: e.dma_start(
                out=bass.AP(sgnT.tensor, 128 * oc + 8 * 128 * D,
                            [[D, 128], [128 * D, 8], [1, 128]]),
                in_=sgnT_sb[wp][:, 1024:2048].rearrange("p (c q) -> p c q", c=8)),
                inc=f"dSgT{2 + wp}", inc_n=16)
            sgn_store[wp] = (s0, s1)
        d_alpha = E("dve", lambda e: e.tensor_scalar(
            c_alpha[:, :], alpha_raw[:, :], 1.0 / D, None, ALU.mult), inc="sDVE")

        wo_mm_last = 0
        wobuf_free = 0
        dense_prev = []
        ffn_tp_last = 0
        sgC = [0]
        lnC_war = {}
        o1_war = {}
        y_war = {}
        lnx1_tp = {}
        out_store = 0
        x1_add_done = 0

        def dense(weight_dram, moving, affine, g_ctx_wait=0, moving_ready=None,
                  out_buf=None, halves=(0, 1), evac_d=None, close=True,
                  init_last=0):
            nonlocal wobuf_free
            last = init_last
            if evac_d is None:
                evac_d = {}
            for half in halves:
                W("sync", "sPE", wobuf_free if half == 0 else last)
                slabs = 0
                for k in range(8):
                    slabs = E("sync", lambda e, k=k, half=half, wd=weight_dram: e.dma_start(
                        out=wobuf[:, k * D:(k + 1) * D],
                        in_=wd[(half * 8 + k) * 128:(half * 8 + k + 1) * 128, :]),
                        inc="dWO", inc_n=16)
                for m in range(NKC):
                    bank = m % 2
                    if evac_d.get((half, m - 2)):
                        W("pe", evac_d[(half, m - 2)][0], evac_d[(half, m - 2)][1])
                    elif half == 1 and evac_d.get((0, NKC - 2 + m)):
                        W("pe", evac_d[(0, NKC - 2 + m)][0], evac_d[(0, NKC - 2 + m)][1])
                    elif half == 0 and dense_prev:
                        W("pe", dense_prev[-1][0], dense_prev[-1][1])
                    W("pe", "dWO", slabs)
                    if g_ctx_wait:
                        W("pe", "dCtx", g_ctx_wait)
                    if moving_ready is not None and m == 0:
                        W("pe", "sDVE", moving_ready[half * 8 + 7])
                    for k in range(8):
                        kc2 = half * 8 + k
                        mmv = E("pe", lambda e, k=k, m=m, bank=bank, kc2=kc2: e.matmul(
                            ps_mm[bank][:, :],
                            wobuf[:, k * D + m * 128: k * D + (m + 1) * 128],
                            moving[:, kc2 * 512:(kc2 + 1) * 512],
                            start=(k == 0), stop=(k == 7)),
                            inc=("sPE" if k == 7 else None))
                    last = mmv
                    if half == 0:
                        W("act", "sPE", mmv)
                        ev = E("act", lambda e, m=m, bank=bank, ob=out_buf: e.copy(
                            ob[:, m * 512:(m + 1) * 512], ps_mm[bank][:, :]),
                            inc="sACT", ser=False)
                        evac_d[(half, m)] = ("sACT", ev)
                    else:
                        W("dve", "sPE", mmv)
                        ev = E("dve", lambda e, m=m, bank=bank, ob=out_buf: e.tensor_tensor(
                            ob[:, m * 512:(m + 1) * 512],
                            ob[:, m * 512:(m + 1) * 512],
                            ps_mm[bank][:, :], ALU.add),
                            inc=(None if affine else "sDVE"))
                        if affine:
                            ev = E("dve", lambda e, m=m, ob=out_buf: e.tensor_scalar(
                                ob[:, m * 512:(m + 1) * 512],
                                ob[:, m * 512:(m + 1) * 512],
                                c_alpha[:, m:m + 1], c_bias[:, m:m + 1],
                                ALU.mult, ALU.add), inc="sDVE")
                        evac_d[(half, m)] = ("sDVE", ev)
            wobuf_free = last
            if close:
                dense_prev.append(evac_d[(1, NKC - 1)])
                return last, evac_d[(1, NKC - 1)]
            return last, evac_d

        mul_done_prev = [None]
        rl_ev_prev = [None]
        norm_state = {}
        pe_rl_box = [0]
        g_ctx_box = [0]
        o1_tp_last = [0]
        wo_state = {}
        NQI = TH // 512

        def ctx_loads(qi):
            W("gp", "sCC", 1)
            W("gp", "sPE", wo_state.get(("mm", qi - 1), 0))
            for hi in range(2):
                g_ctx_box[0] = E("gp", lambda e, qi=qi, hi=hi: e.dma_start(
                    out=bass.AP(ctx_qc, hi * 8 * 512,
                                [[NKC * 512, 128], [512, 8], [1, 512]]),
                    in_=bass.AP(cc_out.tensor, gp_off[0] + CCR * TH * hi + 512 * qi,
                                [[TH, 128], [128 * TH, 8], [1, 512]])),
                    inc="dCtx", inc_n=16)

        def norm_chunks(qi, c2s):
            st = norm_state.setdefault(qi, {"mul": {}, "rl": {}})
            for c2 in c2s:
                rpar = c2 % 2
                hi = c2 // 8
                if c2 % 8 == 0:
                    if hi == 1:
                        W("gp", "sPE", pe_rl_box[0])
                    g_ctx_box[0] = E("gp", lambda e, qi=qi, hi=hi: e.dma_start(
                        out=rl2h[0:2, :].rearrange("p (c q) -> p c q", c=8),
                        in_=bass.AP(cc_out.tensor,
                                    gp_off2[0] + CCR * TH * hi + D * TH + 512 * qi,
                                    [[TH, 2], [2 * TH, 8], [1, 512]])),
                        inc="dCtx", inc_n=16)
                W("pe", "dCtx", g_ctx_box[0])
                W("pe", "dIdent", g_ind2)
                if c2 >= 2:
                    W("pe", "sACT", st["rl"][c2 - 2])
                elif rl_ev_prev[0] is not None:
                    W("pe", "sACT", rl_ev_prev[0][14 + c2])
                pe_rl_box[0] = E("pe", lambda e, c2=c2, rpar=rpar: e.matmul(
                    ps_rl[rpar][:, :], c_ind2[:, :],
                    rl2h[0:2, (c2 % 8) * 512:(c2 % 8 + 1) * 512],
                    start=True, stop=True), inc="sPE")
                W("act", "sPE", pe_rl_box[0])
                if c2 >= 2:
                    W("act", "sDVE", st["mul"][c2 - 2])
                elif mul_done_prev[0] is not None:
                    W("act", "sDVE", mul_done_prev[0][14 + c2])
                st["rl"][c2] = E("act", lambda e, rpar=rpar: e.copy(
                    rl_full[rpar][:, :], ps_rl[rpar][:, :]), inc="sACT", ser=False)
                W("dve", "sACT", st["rl"][c2])
                st["mul"][c2] = E("dve", lambda e, c2=c2, rpar=rpar: e.tensor_mul(
                    ctx_qc[:, c2 * 512:(c2 + 1) * 512],
                    ctx_qc[:, c2 * 512:(c2 + 1) * 512],
                    rl_full[rpar][:, :]), inc="sDVE", ser=False)
            if len(st["mul"]) == 16:
                mul_done_prev[0] = st["mul"]
                rl_ev_prev[0] = st["rl"]

        def wo_half(qi, half):
            st = norm_state[qi]
            if half == 0:
                W("act", "sPE", o1_tp_last[0])
            ev = wo_state.get(("evac", qi))
            last, evac_d = dense(wo_bf, ctx_qc, affine=False,
                                 g_ctx_wait=g_ctx_box[0], moving_ready=st["mul"],
                                 out_buf=big_ft, halves=(half,),
                                 evac_d=ev, close=False,
                                 init_last=wo_state.get(("mm", qi), 0))
            wo_state[("evac", qi)] = evac_d
            wo_state[("mm", qi)] = last
            if half == 1:
                dense_prev.append(evac_d[(1, NKC - 1)])
                wo_state[("ev", qi)] = evac_d[(1, NKC - 1)]

        ctx_loads(0)
        norm_chunks(0, range(16))
        wo_half(0, 0)
        wo_half(0, 1)

        for qi in range(NQI):
            wo_ev = wo_state[("ev", qi)]
            if debug and qi == 0:
                p.barrier(p.snap())
                E("sync", lambda e: e.dma_start(out=dbg["dbg_o1"], in_=big_ft[:, :]),
                  inc="dDbg1", inc_n=16)
                p.barrier(p.snap())

            for ti in range(4):
                W("sync", "sDVE", x1_add_done)
                d_x = E("sync", lambda e, qi=qi, ti=ti: e.dma_start(
                    out=x_t[:, :],
                    in_=x_own[512 * qi + 128 * ti: 512 * qi + 128 * ti + 128, :]),
                    inc="dXC", inc_n=16)
                tp = ti % 2
                W("pe", wo_ev[0], wo_ev[1])
                W("pe", "sDVE", o1_war.get(tp, 0))
                tp_block(lambda kc, ti=ti: big_ft[:, kc * 512 + 128 * ti: kc * 512 + 128 * ti + 128],
                         lambda e, g, bank, tp=tp: e.copy(
                             o1_t[tp][:, 512 * g:512 * (g + 1)], bank[:, :]),
                         ps_tc, c_ident_bf, NKC)
                if ti == 3:
                    o1_tp_last[0] = p.cnt["sPE"]
                o1_ready = tp_hist[(NKC - 1) % 2]
                W("dve", "sACT", o1_ready)
                rstd2, negmr2 = ln_stats(o1_t[tp][:, :], sgC[0], stats2, sq_dump2,
                                         war=lnC_war.get(sgC[0] - 4, 0))
                a_ap = E("dve", lambda e, ti=ti, tp=tp, rstd2=rstd2, negmr2=negmr2: e.tensor_scalar(
                    x1b[:, ti * D:(ti + 1) * D], o1_t[tp][:, :], rstd2, negmr2,
                    ALU.mult, ALU.add), inc="sDVE")
                o1_war[tp] = a_ap
                lnC_war[sgC[0]] = a_ap
                sgC[0] += 1
                W("dve", "dXC", d_x)
                x1_add_done = E("dve", lambda e, ti=ti: e.tensor_add(
                    x1b[:, ti * D:(ti + 1) * D], x1b[:, ti * D:(ti + 1) * D],
                    x_t[:, :]), inc="sDVE")
                rstd3, negmr3 = ln_stats(x1b[:, ti * D:(ti + 1) * D], sgC[0], stats2,
                                         sq_dump2, war=x1_add_done)
                W("dve", "sPE", lnx1_tp.get(tp, 0))
                a_l3 = E("dve", lambda e, ti=ti, tp=tp, rstd3=rstd3, negmr3=negmr3: e.tensor_scalar(
                    lnx1_bf[tp][:, :], x1b[:, ti * D:(ti + 1) * D], rstd3, negmr3,
                    ALU.mult, ALU.add), inc="sDVE")
                lnC_war[sgC[0]] = a_l3
                sgC[0] += 1
                W("pe", "sDVE", a_l3)
                tp_block(lambda kc, tp=tp: lnx1_bf[tp][:, 128 * kc:128 * (kc + 1)],
                         lambda e, g, bank, ti=ti: e.copy(
                             lnx1T[:, :].rearrange("p (c t) -> p c t", c=NKC)
                             [:, 4 * g:4 * g + 4, ti * 128:(ti + 1) * 128],
                             bank[:, :].rearrange("p (c q) -> p c q", c=4)),
                         ps_tc, c_ident_bf, NKC)
                lnx1_tp[tp] = p.cnt["sPE"]

            if debug and qi == 0:
                p.barrier(p.snap())
                E("sync", lambda e: e.dma_start(out=dbg["dbg_x1"], in_=x1b[:, :]),
                  inc="dDbg2", inc_n=16)
                p.barrier(p.snap())
            for s_ in ("dSgT", "dSgT1", "dSgT2", "dSgT3"):
                W("sync", s_, p.cnt.get(s_, 0))
            W("act", "sPE", ffn_tp_last)
            ffn_mm_last, ffn_ev = dense(sgnT, lnx1T, affine=True, out_buf=big_fy)
            if debug and qi == 0:
                p.barrier(p.snap())
                E("sync", lambda e: e.dma_start(out=dbg["dbg_y"], in_=big_fy[:, :]),
                  inc="dDbg3", inc_n=16)
                p.barrier(p.snap())

            if qi + 1 < NQI:
                ctx_loads(qi + 1)
            for ti in range(4):
                tp = ti % 2
                W("pe", ffn_ev[0], ffn_ev[1])
                W("pe", "sDVE", y_war.get(tp, 0))
                tp_block(lambda kc, ti=ti: big_fy[:, kc * 512 + 128 * ti: kc * 512 + 128 * ti + 128],
                         lambda e, g, bank, tp=tp: e.copy(
                             y_t[tp][:, 512 * g:512 * (g + 1)], bank[:, :]),
                         ps_tc, c_ident_bf, NKC)
                if ti == 3:
                    ffn_tp_last = p.cnt["sPE"]
                y_ready = tp_hist[(NKC - 1) % 2]
                W("dve", "sACT", y_ready)
                rstd4, negmr4 = ln_stats(y_t[tp][:, :], sgC[0], stats2, sq_dump2,
                                         war=lnC_war.get(sgC[0] - 4, 0))
                W("dve", "dOut", out_store)
                a_l4 = E("dve", lambda e, tp=tp, rstd4=rstd4, negmr4=negmr4: e.tensor_scalar(
                    out_t[:, :], y_t[tp][:, :], rstd4, negmr4,
                    ALU.mult, ALU.add), inc="sDVE")
                y_war[tp] = a_l4
                lnC_war[sgC[0]] = a_l4
                sgC[0] += 1
                d_out = E("dve", lambda e, ti=ti: e.tensor_add(
                    out_t[:, :], out_t[:, :], x1b[:, ti * D:(ti + 1) * D]), inc="sDVE")
                W("sync", "sDVE", d_out)
                out_store = E("sync", lambda e, qi=qi, ti=ti: e.dma_start(
                    out=out_own[512 * qi + 128 * ti: 512 * qi + 128 * ti + 128, :],
                    in_=out_t[:, :]), inc="dOut", inc_n=16)
                if qi + 1 < NQI:
                    norm_chunks(qi + 1, range(4 * ti, 4 * ti + 4))
                    if ti == 2:
                        wo_half(qi + 1, 0)
            if qi + 1 < NQI:
                wo_half(qi + 1, 1)

        for s in list(p.cnt):
            if not s.startswith("d"):
                continue
            q = "gp" if s in ("dIdent", "dWqkv", "dWoCast", "dCC", "dCtx") else "sync"
            W(q, s, p.cnt[s])

        block = st_root.enter_context(nc.Block())

        @block.tensor
        def _(tensor):
            p.replay("pe", tensor, sems)

        @block.scalar
        def _(scalar):
            p.replay("act", scalar, sems)

        @block.vector
        def _(vector):
            p.replay("dve", vector, sems)

        @block.sync
        def _(sync):
            p.replay("sync", sync, sems)

        @block.gpsimd
        def _(gpsimd):
            p.replay("gp", gpsimd, sems)

    return nc


def _host_prep(inputs):
    x = np.asarray(inputs["L0_attn_x"], dtype=np.float32)
    Wq = np.asarray(inputs["L0_attn_Wq"], dtype=np.float32)
    Wk = np.asarray(inputs["L0_attn_Wk"], dtype=np.float32)
    Wv = np.asarray(inputs["L0_attn_Wv"], dtype=np.float32)
    Wo = np.asarray(inputs["L0_attn_Wo"], dtype=np.float32)
    Wf = np.asarray(inputs["L0_ffn_weight"], dtype=np.float32)
    bias = np.asarray(inputs["L0_ffn_bias"], dtype=np.float32)

    j = np.arange(32, dtype=np.float64)
    theta = 10000.0 ** (-2.0 * j / 64.0)
    pos = np.arange(T, dtype=np.float64)
    ang = np.outer(pos, theta)
    cosd, sind = np.cos(ang), np.sin(ang)
    crep = np.empty((128, T), dtype=np.float32)
    srep = np.empty((128, T), dtype=np.float32)
    for p_ in range(128):
        o = ORIG64[p_ % 64]
        if o < 32:
            crep[p_] = cosd[:, o]
            srep[p_] = -sind[:, o]
        else:
            crep[p_] = cosd[:, o - 32]
            srep[p_] = sind[:, o - 32]

    tk = np.arange(128)[:, None]
    tq = np.arange(128)[None, :]
    maskt = np.where(tk <= tq, 0.0, MASKVAL).astype(np.float32)
    identf = np.eye(128, dtype=np.float32)
    ind2 = np.zeros((2, 128), dtype=np.float32)
    ind2[0, 0:64] = 1.0
    ind2[1, 64:128] = 1.0
    consts = np.broadcast_to(
        np.array([[EPS, -0.5, 0.125, 1.0 / D, -1.0, 0.5, 0.0, 0.0]], dtype=np.float32),
        (128, 8)).copy()
    bias_t = np.ascontiguousarray(bias.reshape(16, 128).T)

    in_maps = []
    for c in range(2 * B):
        b, hh = c // 2, c % 2
        qcols = [64 * (16 * hh + lh) + ORIG64[pp] for lh in range(16) for pp in range(64)]
        kcols = []
        for i in range(4):
            base = 64 * (4 * hh + i)
            blk = [base + ORIG64[pp] for pp in range(64)]
            kcols += blk + blk
        vcols = [64 * (4 * hh + i) + d_ for i in range(4) for d_ in range(64)]
        in_maps.append({
            "x_full": np.ascontiguousarray(x[b]),
            "x_own": np.ascontiguousarray(x[b, TH * hh:TH * (hh + 1)]),
            "wq": np.ascontiguousarray(Wq[:, qcols]),
            "wk": np.ascontiguousarray(Wk[:, kcols]),
            "wv": np.ascontiguousarray(Wv[:, vcols]),
            "wo": Wo,
            "wff": Wf,
            "bias_t": bias_t,
            "crep": crep.astype(ml_dtypes.bfloat16),
            "srep": srep.astype(ml_dtypes.bfloat16),
            "maskt": maskt,
            "identf": identf,
            "consts": consts,
            "coreoff": np.array([[hh * 1024 * TH, hh * 16 * TH]], dtype=np.int32),
            "ind2": ind2,
        })
    return in_maps


def kernel(**inputs):
    in_maps = _host_prep(inputs)
    nc = build_nc()
    res = run_bass_kernel_spmd(nc, in_maps, core_ids=list(range(8)))
    out = np.empty((B, T, D), dtype=np.float32)
    for c in range(8):
        b, hh = c // 2, c % 2
        out[b, TH * hh:TH * (hh + 1)] = np.asarray(res.results[c]["out_own"])
    return out
